# version 10
# baseline (speedup 1.0000x reference)
import contextlib
import numpy as np
import concourse.bass as bass
import concourse.mybir as mybir
from concourse.bass_utils import run_bass_kernel_spmd

F32 = mybir.dt.float32
BF16 = mybir.dt.bfloat16
AF = mybir.ActivationFunctionType
ALU = mybir.AluOpType
AX = mybir.AxisListType

CENGS = ("tensor", "vector", "scalar", "gpsimd", "sync")
QUEUES = ("sync", "scalar", "gpsimd")
NSLOT = 3

D = 1024
T_FULL = 8192
T_HALF = 4096
LN_EPS = 1e-5



class Prog:
    def __init__(self, nc):
        self.nc = nc
        self._es = contextlib.ExitStack()
        self.q = {e: [] for e in CENGS}
        self.sems = {}
        self.cnt = {}
        self.marks = {}
        self.drams = {}
        for e in CENGS:
            self._mksem("c_" + e)
        self.qslot = {}
        for q in QUEUES:
            self.qslot[q] = 0
            for i in range(NSLOT):
                self._mksem("q_%s%d" % (q, i))
        self._mksem("bar")

    def _mksem(self, name):
        self.sems[name] = self._es.enter_context(self.nc.semaphore(name))
        self.cnt[name] = 0

    def sb(self, name, shape, dtype):
        return self._es.enter_context(self.nc.sbuf_tensor(name, shape, dtype))

    def psum(self, name, shape, dtype=F32):
        return self._es.enter_context(self.nc.psum_tensor(name, shape, dtype))

    def dram(self, name, shape, dtype=F32, kind="Internal"):
        t = self.nc.dram_tensor(name, shape, dtype, kind=kind)
        self.drams[name] = t
        return t.ap()

    def _collect_waits(self, consumer, deps, selfwait):
        own = "c_" + consumer
        waits = []
        for d in deps:
            if d is None:
                continue
            sem, tgt = d
            if sem == own and not selfwait:
                continue
            key = (consumer, sem)
            if self.marks.get(key, -1) >= tgt:
                continue
            self.marks[key] = tgt
            waits.append((self.sems[sem], tgt))
        return waits

    def op(self, eng, fn, deps=(), selfwait=True):
        sem = "c_" + eng
        waits = self._collect_waits(eng, deps, selfwait)
        h = self.sems[sem]

        def run(e, fn=fn, waits=waits, h=h):
            for wh, wt in waits:
                e.wait_ge(wh, wt)
            inst = fn(e)
            inst.then_inc(h, 1)
        self.q[eng].append(run)
        self.cnt[sem] += 1
        return (sem, self.cnt[sem])

    def _dma_common(self, queue, deps):
        slot = self.qslot[queue]
        self.qslot[queue] = (slot + 1) % NSLOT
        sem = "q_%s%d" % (queue, slot)
        alldeps = list(deps)
        if self.cnt[sem] > 0:
            alldeps.append((sem, self.cnt[sem]))
        waits = self._collect_waits(queue, alldeps, selfwait=True)
        return sem, waits

    def dma(self, queue, out, in_, deps=(), accum=False, allow_slow=False):
        sem, waits = self._dma_common(queue, deps)
        h = self.sems[sem]

        def run(e, out=out, in_=in_, waits=waits, h=h):
            for wh, wt in waits:
                e.wait_ge(wh, wt)
            kw = {}
            if accum:
                kw["accum_op"] = ALU.add
            if allow_slow:
                kw["allow_slow_non_contiguous"] = True
            e.dma_start(out=out, in_=in_, **kw).then_inc(h, 16)
        self.q[queue].append(run)
        self.cnt[sem] += 16
        return (sem, self.cnt[sem])

    def dmaT(self, queue, out, in_, deps=()):
        sem, waits = self._dma_common(queue, deps)
        h = self.sems[sem]

        def run(e, out=out, in_=in_, waits=waits, h=h):
            for wh, wt in waits:
                e.wait_ge(wh, wt)
            e.dma_start_transpose(out, in_).then_inc(h, 16)
        self.q[queue].append(run)
        self.cnt[sem] += 16
        return (sem, self.cnt[sem])

    def barrier(self):
        deps = [("c_" + e, self.cnt["c_" + e]) for e in CENGS]
        deps += [("q_%s%d" % (q, i), self.cnt["q_%s%d" % (q, i)])
                 for q in QUEUES for i in range(NSLOT)]
        h = self.sems["bar"]
        self.cnt["bar"] += len(CENGS)
        tgt = self.cnt["bar"]
        for e in CENGS:
            waits = self._collect_waits(e, deps, selfwait=False)

            def run(eng, waits=waits, h=h, tgt=tgt):
                for wh, wt in waits:
                    eng.wait_ge(wh, wt)
                eng.nop().then_inc(h, 1)
                eng.wait_ge(h, tgt)
            self.q[e].append(run)
            for (sem, t) in deps:
                self.marks[(e, sem)] = max(self.marks.get((e, sem), -1), t)
        return ("bar", tgt)

    def replay(self):
        with self.nc.Block() as block:
            @block.sync
            def _(e):
                for fn in self.q["sync"]:
                    fn(e)

            @block.scalar
            def _(e):
                for fn in self.q["scalar"]:
                    fn(e)

            @block.vector
            def _(e):
                for fn in self.q["vector"]:
                    fn(e)

            @block.tensor
            def _(e):
                for fn in self.q["tensor"]:
                    fn(e)

            @block.gpsimd
            def _(e):
                for fn in self.q["gpsimd"]:
                    fn(e)

    def close(self):
        self._es.close()



class Arena:
    def __init__(self, p):
        self.wa = p.sb("wa", [128, 32768], BF16)
        self.fa = p.sb("fa", [128, 8192], F32)
        self.fb = p.sb("fb", [128, 4096], F32)
        self.ba = p.sb("ba", [128, 16384], BF16)
        self.bb = p.sb("bb", [128, 8192], BF16)
        self.fs = p.sb("fs", [128, 3072], F32)
        self.ms = p.sb("ms", [128, 1024], F32)
        self.eye = p.sb("eye", [128, 128], BF16)
        self.eyef = p.sb("eyef", [128, 128], F32)
        self.ones = p.sb("ones", [1, 512], BF16)
        self.br = p.sb("br", [1, 4096], BF16)
        self.brf = p.sb("brf", [1, 4096], F32)
        self.cf = p.sb("cf", [128, 8], F32)
        self.pa = p.psum("pa", [128, 3072], F32)
        self.pb = p.psum("pb", [128, 2048], BF16)

    def load_consts(self, p, dr):
        p.dma("gpsimd", self.eye[:, :], dr["c_eye"])
        p.dma("sync", self.eyef[:, :], dr["c_eye"])
        p.dma("gpsimd", self.ones[0:1, :], dr["c_ones"])
        p.op("vector", lambda e: e.memset(self.cf[:, 0:1], LN_EPS))
        p.barrier()



def mm_accum(p, out_psum, pairs, deps=()):
    def f(e):
        last = None
        n = len(pairs)
        for i, (l, r) in enumerate(pairs):
            last = e.matmul(out_psum, l, r, start=(i == 0), stop=(i == n - 1))
        return last
    return p.op("tensor", f, deps=deps, selfwait=False)


def tr128(p, ar, dst, src, deps, slot, prev_evac, evac="vector"):
    pb = ar.pb
    tdeps = list(deps)
    if prev_evac is not None:
        tdeps.append(prev_evac)
    t1 = p.op("tensor",
              lambda e, dst=dst, src=src, slot=slot:
              e.transpose(pb[:, slot * 1024:slot * 1024 + 128], src, ar.eye[:, :]),
              deps=tdeps, selfwait=False)
    t2 = p.op(evac,
              lambda e, dst=dst, slot=slot:
              e.tensor_copy(dst, pb[:, slot * 1024:slot * 1024 + 128]),
              deps=[t1], selfwait=False)
    return t2


def ln_chunk(p, src, dst, stats, dep, nsub=4, width=D, eps_ap=None,
             out_deps=()):
    tsum = [p.op("vector",
                 lambda e, i=i: e.reduce_sum(
                     stats[:, 6 * i:6 * i + 1],
                     src[:, i * width:(i + 1) * width], axis=AX.X),
                 deps=[dep] if i == 0 else (), selfwait=False)
            for i in range(nsub)]
    tsq = [p.op("scalar",
                lambda e, i=i: e.activation(
                    dst[:, i * width:(i + 1) * width],
                    src[:, i * width:(i + 1) * width],
                    AF.Square, accum_out=stats[:, 6 * i + 1:6 * i + 2]),
                deps=([dep] + list(out_deps)) if i == 0 else (),
                selfwait=False)
           for i in range(nsub)]
    tmu = [p.op("gpsimd",
                lambda e, i=i: e.tensor_scalar(
                    stats[:, 6 * i + 2:6 * i + 3], stats[:, 6 * i:6 * i + 1],
                    -1.0 / width, None, ALU.mult),
                deps=[tsum[i]], selfwait=False) for i in range(nsub)]
    tv = [p.op("vector",
               lambda e, i=i: e.scalar_tensor_tensor(
                   stats[:, 6 * i + 3:6 * i + 4], stats[:, 6 * i:6 * i + 1],
                   stats[:, 6 * i + 2:6 * i + 3], stats[:, 6 * i + 1:6 * i + 2],
                   ALU.mult, ALU.add),
               deps=[tsum[i], tsq[i], tmu[i]], selfwait=True)
          for i in range(nsub)]
    tr = []
    for i in range(nsub):
        tl = p.op("scalar",
                  lambda e, i=i: e.activation(
                      stats[:, 6 * i + 4:6 * i + 5],
                      stats[:, 6 * i + 3:6 * i + 4],
                      AF.Ln, bias=eps_ap, scale=1.0 / width),
                  deps=[tv[i]], selfwait=True)
        tr.append(p.op("scalar",
                       lambda e, i=i: e.activation(
                           stats[:, 6 * i + 5:6 * i + 6],
                           stats[:, 6 * i + 4:6 * i + 5],
                           AF.Exp, scale=-0.5),
                       deps=[tl], selfwait=True))
    outs = [p.op("vector",
                 lambda e, i=i: e.tensor_scalar(
                     dst[:, i * width:(i + 1) * width],
                     src[:, i * width:(i + 1) * width],
                     stats[:, 6 * i + 2:6 * i + 3],
                     stats[:, 6 * i + 5:6 * i + 6],
                     ALU.add, ALU.mult),
                 deps=[tr[i]] + (list(out_deps) if i == 0 else []),
                 selfwait=(i == 0)) for i in range(nsub)]
    return outs[-1]


def load_wT_bf16(p, dst, w_hbm, kdim, ndim, deps=(), col0=0):
    toks = []
    nk = kdim // 128
    for j in range(nk):
        toks.append(p.dma("gpsimd", dst[:, col0 + j * ndim: col0 + (j + 1) * ndim],
                          w_hbm[j * 128:(j + 1) * 128, :], deps=deps))
    return toks


def row2d(ap):
    return ap.rearrange("(a n) -> a n", a=1)


def fold_ln_into_w(p, ar, wsb, nk, ndim, g_tile, b_tile, bias_hbm, brow_bf,
                   brow_cols, deps, col0=0):
    tb = None
    if bias_hbm is not None:
        tb = p.dma("sync", ar.brf[0:1, 0:ndim], row2d(bias_hbm), deps=())
    nslice = ndim // 512
    row_toks = []
    prev_evac = None
    for s in range(nslice):
        pairs = [(b_tile[:, j:j + 1],
                  wsb[:, col0 + j * ndim + s * 512: col0 + j * ndim + s * 512 + 512])
                 for j in range(nk)]
        mdeps = list(deps) + ([prev_evac] if prev_evac else [])
        tmm = mm_accum(p, ar.pa[0:1, 0:512], pairs, deps=mdeps)
        if bias_hbm is not None:
            tev = p.op("vector",
                       lambda e, s=s: e.tensor_add(
                           ar.brf[0:1, s * 512:(s + 1) * 512],
                           ar.brf[0:1, s * 512:(s + 1) * 512],
                           ar.pa[0:1, 0:512]),
                       deps=[tmm, tb], selfwait=False)
            tcast = p.op("vector",
                         lambda e, s=s: e.tensor_copy(
                             brow_bf[0:1, brow_cols + s * 512:
                                     brow_cols + (s + 1) * 512],
                             ar.brf[0:1, s * 512:(s + 1) * 512]),
                         deps=[tev], selfwait=True)
        else:
            tcast = p.op("vector",
                         lambda e, s=s: e.tensor_copy(
                             brow_bf[0:1, brow_cols + s * 512:
                                     brow_cols + (s + 1) * 512],
                             ar.pa[0:1, 0:512]),
                         deps=[tmm], selfwait=False)
        prev_evac = tcast
        row_toks.append(tcast)
    w_toks = []
    for j in range(nk):
        w_toks.append(p.op("vector",
                           lambda e, j=j: e.tensor_scalar(
                               wsb[:, col0 + j * ndim: col0 + (j + 1) * ndim],
                               wsb[:, col0 + j * ndim: col0 + (j + 1) * ndim],
                               g_tile[:, j:j + 1], None, ALU.mult),
                           deps=row_toks, selfwait=False))
    return w_toks, row_toks[-1]


def stream_chunk_ap(s_ap, t0, ntok=512, width=D):
    return s_ap[t0:t0 + ntok, :].rearrange("(n p) d -> p n d", p=128)


def sb3(tile_slice, width):
    return tile_slice.rearrange("p (n d) -> p n d", d=width)


def fm_block_ap(s_ap, r0, nrow_chunks, c0, ncol):
    return s_ap[r0:r0 + 128 * nrow_chunks, c0:c0 + ncol].rearrange(
        "(c p) t -> p c t", p=128)



def phase_proj_fm(p, ar, dr, t0, t1, src_key, dst_key, w_ap, g_ap, b_ap,
                  bias_ap, act, stage=99):
    wtoks = load_wT_bf16(p, ar.wa, w_ap, D, 4096)
    tg = p.dma("sync", ar.fs[:, 0:8], g_ap)
    tb = p.dma("gpsimd", ar.bb[:, 0:8], b_ap)
    if stage == 0:
        p.barrier()
        return
    w_toks, brow_tok = fold_ln_into_w(
        p, ar, ar.wa, 8, 4096, ar.fs[:, 0:8], ar.bb[:, 0:8],
        bias_ap, ar.br, 0, deps=wtoks + [tg, tb])
    p.barrier()
    if stage == 1:
        return

    prev_ln = [None, None]
    prev_tr = None
    prev_mm = None
    hdmas_tail = [None] * 4
    evac_of_slot = [None, None, None, None]
    for ck in range((t1 - t0) // 512):
        tt = t0 + ck * 512
        par = ck % 2
        xt = ar.fa[:, 0:4096] if par == 0 else ar.fb[:, 0:4096]
        tX = p.dma("sync", sb3(xt, D), stream_chunk_ap(dr[src_key], tt),
                   deps=[prev_ln[par]])
        tLN = ln_chunk(p, xt, ar.ba[:, 0:4096],
                       ar.ms[:, 24 * par:24 * par + 24], tX,
                       eps_ap=ar.cf[:, 0:1], out_deps=[prev_tr])
        prev_ln[par] = tLN
        ev = []
        prev = [None, None]
        for i in range(4):
            for j in range(8):
                k = i * 8 + j
                tdeps = [tLN]
                if k < 2 and prev_mm is not None:
                    tdeps.append(prev_mm)
                tev = tr128(p, ar,
                            ar.bb[:, j * 512 + i * 128: j * 512 + i * 128 + 128],
                            ar.ba[:, i * 1024 + j * 128: i * 1024 + j * 128 + 128],
                            tdeps, k % 2, prev[k % 2])
                prev[k % 2] = tev
                ev.append(tev)
        prev_tr = ev[-1]
        if stage == 2:
            p.barrier()
            continue
        hdmas = []
        for eg in range(4):
            htoks = []
            if act == "silu":
                hbase = 4096 + (eg % 2) * 4096
            else:
                hbase = 4096
            for el in range(8):
                ec = eg * 8 + el
                pairs = [(ar.wa[:, j * 4096 + ec * 128: j * 4096 + ec * 128 + 128],
                          ar.bb[:, j * 512:(j + 1) * 512]) for j in range(8)]
                pairs.append((ar.br[0:1, ec * 128:ec * 128 + 128],
                              ar.ones[0:1, 0:512]))
                mdeps = ev + w_toks + [brow_tok]
                if evac_of_slot[ec % 4] is not None:
                    mdeps.append(evac_of_slot[ec % 4])
                tmm = mm_accum(p, ar.pa[:, (ec % 4) * 512:(ec % 4) * 512 + 512],
                               pairs, deps=mdeps)
                sdeps = [tmm]
                if act == "silu":
                    sdeps.append(hdmas_tail[(eg + 2) % 4])
                    tsil = p.op("scalar",
                                lambda e, ec=ec, el=el, hbase=hbase: e.activation(
                                    ar.ba[:, hbase + el * 512: hbase + el * 512 + 512],
                                    ar.pa[:, (ec % 4) * 512:(ec % 4) * 512 + 512],
                                    AF.Silu),
                                deps=sdeps, selfwait=False)
                else:
                    sdeps.append(hdmas_tail[(eg + 3) % 4])
                    tsil = p.op("scalar",
                                lambda e, ec=ec, el=el, hbase=hbase: e.copy(
                                    ar.fa[:, hbase + el * 512: hbase + el * 512 + 512],
                                    ar.pa[:, (ec % 4) * 512:(ec % 4) * 512 + 512]),
                                deps=sdeps, selfwait=False)
                evac_of_slot[ec % 4] = tsil
                htoks.append(tsil)
                prev_mm = tmm
            if act == "silu":
                hd = p.dma("sync",
                           fm_block_ap(dr[dst_key], eg * 1024, 8, tt, 512),
                           sb3(ar.ba[:, hbase:hbase + 4096], 512),
                           deps=htoks)
            else:
                hd = p.dma("sync",
                           fm_block_ap(dr[dst_key], eg * 1024, 8, tt, 512),
                           sb3(ar.fa[:, 4096:8192], 512), deps=htoks)
            hdmas.append(hd)
            hdmas_tail[eg] = hd
    p.barrier()
    return


def phase_mlp_a(p, ar, dr, t0, t1, stage=99):
    phase_proj_fm(p, ar, dr, t0, t1, "s_in", "s_h", dr["w1T"], dr["g1c"],
                  dr["b1c"], dr["bias1"], "silu", stage=stage)



def phase_ln_stream(p, ar, dr, t0, t1, src="s_in", dst="s_out"):
    prev_ln = [None, None]
    prev_st = None
    for ck in range((t1 - t0) // 512):
        tt = t0 + ck * 512
        par = ck % 2
        xt = ar.fa[:, par * 4096:par * 4096 + 4096]
        tX = p.dma("sync", sb3(xt, D), stream_chunk_ap(dr[src], tt),
                   deps=[prev_ln[par]])
        tLN = ln_chunk(p, xt, ar.fb[:, 0:4096],
                       ar.ms[:, 24 * par:24 * par + 24], tX,
                       eps_ap=ar.cf[:, 0:1], out_deps=[prev_st])
        prev_ln[par] = tLN
        prev_st = p.dma("scalar", stream_chunk_ap(dr[dst], tt),
                        sb3(ar.fb[:, 0:4096], D), deps=[tLN])
    p.barrier()



def phase_add_ln(p, ar, dr, t0, t1, x="s_x", slot="s_slot", dst="s_out",
                 dst_off=0):
    prev_ln = None
    prev_add = None
    prev_st = None
    for ck in range((t1 - t0) // 512):
        tt = t0 + ck * 512
        tS = p.dma("sync", sb3(ar.fa[:, 0:4096], D),
                   stream_chunk_ap(dr[slot], tt), deps=[prev_ln])
        tX = p.dma("sync", sb3(ar.fb[:, 0:4096], D),
                   stream_chunk_ap(dr[x], tt), deps=[prev_st])
        tLN = ln_chunk(p, ar.fa[:, 0:4096], ar.fa[:, 4096:8192],
                       ar.ms[:, 0:24], tS, eps_ap=ar.cf[:, 0:1],
                       out_deps=[prev_add])
        prev_ln = tLN
        adds = [p.op("vector",
                     lambda e, i=i: e.tensor_add(
                         ar.fb[:, i * 1024:(i + 1) * 1024],
                         ar.fb[:, i * 1024:(i + 1) * 1024],
                         ar.fa[:, 4096 + i * 1024: 4096 + (i + 1) * 1024]),
                     deps=[tLN, tX], selfwait=True) for i in range(4)]
        prev_add = adds[-1]
        prev_st = p.dma("scalar", stream_chunk_ap(dr[dst], tt - dst_off),
                        sb3(ar.fb[:, 0:4096], D), deps=adds)
    p.barrier()



def phase_mlp_b(p, ar, dr, t0, t1):
    wtoks = load_wT_bf16(p, ar.wa, dr["w2T"], 4096, D)
    tb2 = p.dma("gpsimd", ar.br[0:1, 0:D], row2d(dr["bias2"]))
    p.barrier()

    prev_evac = [None, None, None, None]
    prev_mm_last = None
    prev_adds = None
    prev_st = None
    for ck in range((t1 - t0) // 512):
        tt = t0 + ck * 512
        tH = p.dma("sync", sb3(ar.ba[:, 0:32 * 512], 512),
                   fm_block_ap(dr["s_h"], 0, 32, tt, 512),
                   deps=[prev_mm_last])
        tX = p.dma("sync", sb3(ar.fa[:, 0:4096], D),
                   stream_chunk_ap(dr["s_in"], tt), deps=[prev_adds])
        otoks = []
        for ts_ in range(4):
            for nh in range(2):
                slot = (ts_ % 2) * 2 + nh
                pairs = [(ar.ba[:, ec * 512 + ts_ * 128: ec * 512 + ts_ * 128 + 128],
                          ar.wa[:, ec * 1024 + nh * 512: ec * 1024 + nh * 512 + 512])
                         for ec in range(32)]
                pairs.append((ar.ones[0:1, 0:128],
                              ar.br[0:1, nh * 512:(nh + 1) * 512]))
                mdeps = wtoks + [tH, tb2, tX]
                if prev_evac[slot] is not None:
                    mdeps.append(prev_evac[slot])
                tmm = mm_accum(p, ar.pa[:, slot * 512: slot * 512 + 512],
                               pairs, deps=mdeps)
                prev_mm_last = tmm
                tev = p.op("vector",
                           lambda e, ts_=ts_, nh=nh, slot=slot: e.tensor_add(
                               ar.fb[:, ts_ * 1024 + nh * 512:
                                     ts_ * 1024 + nh * 512 + 512],
                               ar.pa[:, slot * 512: slot * 512 + 512],
                               ar.fa[:, ts_ * 1024 + nh * 512:
                                     ts_ * 1024 + nh * 512 + 512]),
                           deps=[tmm] + ([prev_st] if (ts_ == 0 and nh == 0)
                                         else []),
                           selfwait=False)
                prev_evac[slot] = tev
                otoks.append(tev)
        prev_adds = otoks[-1]
        prev_st = p.dma("scalar", stream_chunk_ap(dr["s_out"], tt),
                        sb3(ar.fb[:, 0:4096], D), deps=otoks)
    p.barrier()
    return



def phase_att_setup(p, ar, dr, l):
    wq = load_wT_bf16(p, ar.wa, dr["w_qkvF"][l * 1024:(l + 1) * 1024, :],
                      D, 3072, col0=0)
    wo = load_wT_bf16(p, ar.wa, dr["w_outF"][l * 1024:(l + 1) * 1024, :],
                      D, 1024, col0=24576)
    tg = p.dma("sync", ar.fs[:, 0:8], dr["l0g"][:, l * 8:(l + 1) * 8])
    tb = p.dma("gpsimd", ar.bb[:, 0:8], dr["l0b"][:, l * 8:(l + 1) * 8])
    w_toks, brow_tok = fold_ln_into_w(
        p, ar, ar.wa, 8, 3072, ar.fs[:, 0:8], ar.bb[:, 0:8],
        None, ar.br, 0, deps=wq + [tg, tb])
    p.dma("gpsimd", ar.br[0:1, 3072:4096], dr["b_outF"][l:l + 1, :])
    p.op("vector", lambda e: e.memset(ar.fs[:, 8:16], -1e30))
    p.op("vector", lambda e: e.memset(ar.fs[:, 16:24], 0.0))
    p.op("vector", lambda e: e.memset(ar.fa[:, :], 0.0))
    p.barrier()


def phase_att_pass1(p, ar, dr, t0, t1):
    first = True
    pv = {}
    gprev = [None, None]
    for ck in range((t1 - t0) // 512):
        tt = t0 + ck * 512
        tX = p.dma("sync", sb3(ar.fb[:, 0:4096], D),
                   stream_chunk_ap(dr["s_att"], tt),
                   deps=[pv.get("ln")])
        tLN = ln_chunk(p, ar.fb[:, 0:4096], ar.ba[:, 0:4096],
                       ar.ms[:, 0:24], tX, eps_ap=ar.cf[:, 0:1],
                       out_deps=[pv.get("trexp")])
        pv["ln"] = tLN
        ev = []
        prev = [None, None]
        for i in range(4):
            for j in range(8):
                k = i * 8 + j
                tev = tr128(p, ar,
                            ar.bb[:, j * 512 + i * 128: j * 512 + i * 128 + 128],
                            ar.ba[:, i * 1024 + j * 128: i * 1024 + j * 128 + 128],
                            [tLN], k % 2, prev[k % 2])
                prev[k % 2] = tev
                ev.append(tev)
        exp_toks = [None] * 8
        for half in range(2):
            mm_toks = []
            cmax_toks = []
            for jcl in range(4):
                jc = half * 4 + jcl
                pairs = [(ar.wa[:, jk * 3072 + jc * 128: jk * 3072 + jc * 128 + 128],
                          ar.bb[:, jk * 512:(jk + 1) * 512]) for jk in range(8)]
                pairs.append((ar.br[0:1, jc * 128:jc * 128 + 128],
                              ar.ones[0:1, 0:512]))
                mdeps = list(ev)
                if exp_toks[jcl] is not None:
                    mdeps.append(exp_toks[jcl])
                elif pv.get("vev") is not None:
                    mdeps.append(pv["vev"][jcl])
                tmm = mm_accum(p, ar.pa[:, jcl * 512:(jcl + 1) * 512], pairs,
                               deps=mdeps)
                mm_toks.append(tmm)
                cdeps = [tmm]
                if half == 0 and jcl == 0 and pv.get("exp_last") is not None:
                    cdeps.append(pv["exp_last"])
                cmax_toks.append(p.op(
                    "vector",
                    lambda e, jc=jc, jcl=jcl: e.reduce_max(
                        ar.ms[:, 40 + jc:41 + jc],
                        ar.pa[:, jcl * 512:(jcl + 1) * 512], axis=AX.X),
                    deps=cdeps, selfwait=False))
            c0 = 8 + half * 4
            tmn = p.op("vector",
                       lambda e, half=half, c0=c0: e.tensor_max(
                           ar.ms[:, 48 + half * 4:52 + half * 4],
                           ar.fs[:, c0:c0 + 4],
                           ar.ms[:, 40 + half * 4:44 + half * 4]),
                       deps=cmax_toks, selfwait=True)
            tdd = p.op("vector",
                       lambda e, half=half, c0=c0: e.tensor_sub(
                           ar.ms[:, 64 + half * 4:68 + half * 4],
                           ar.fs[:, c0:c0 + 4],
                           ar.ms[:, 48 + half * 4:52 + half * 4]),
                       deps=[tmn], selfwait=True)
            tcm = p.op("vector",
                       lambda e, half=half, c0=c0: e.tensor_copy(
                           ar.fs[:, c0:c0 + 4],
                           ar.ms[:, 48 + half * 4:52 + half * 4]),
                       deps=[tdd], selfwait=True)
            tng = p.op("vector",
                       lambda e, half=half: e.tensor_scalar(
                           ar.ms[:, 72 + half * 4:76 + half * 4],
                           ar.ms[:, 48 + half * 4:52 + half * 4],
                           -1.0, None, ALU.mult),
                       deps=[tmn], selfwait=True)
            tsj = p.op("scalar",
                       lambda e, half=half: e.activation(
                           ar.ms[:, 56 + half * 4:60 + half * 4],
                           ar.ms[:, 64 + half * 4:68 + half * 4], AF.Exp),
                       deps=[tdd], selfwait=False)
            for jcl in range(4):
                jc = half * 4 + jcl
                texp = p.op("scalar",
                            lambda e, jc=jc, jcl=jcl: e.activation(
                                ar.ba[:, jc * 512:(jc + 1) * 512],
                                ar.pa[:, jcl * 512:(jcl + 1) * 512],
                                AF.Exp, bias=ar.ms[:, 72 + jc:73 + jc],
                                accum_out=ar.ms[:, 80 + jc:81 + jc]),
                            deps=[tng, mm_toks[jcl]], selfwait=True)
                exp_toks[jc] = texp
            if half == 0:
                sj_tok0 = tsj
            else:
                sj_tok1 = tsj
        pv["exp_last"] = exp_toks[7]
        tz1 = p.op("vector",
                   lambda e: e.tensor_mul(ar.fs[:, 16:24], ar.fs[:, 16:24],
                                          ar.ms[:, 56:64]),
                   deps=[sj_tok0, sj_tok1], selfwait=True)
        tz2 = p.op("vector",
                   lambda e: e.tensor_add(ar.fs[:, 16:24], ar.fs[:, 16:24],
                                          ar.ms[:, 80:88]),
                   deps=[tz1] + exp_toks, selfwait=True)
        ev2 = []
        prev2 = [None, None]
        for i in range(4):
            for jc in range(8):
                k = i * 8 + jc
                tdeps = [exp_toks[jc]]
                if k < 2 and pv.get("gmm") is not None:
                    tdeps.append(pv["gmm"])
                tev = tr128(p, ar,
                            ar.ba[:, 8192 + i * 1024 + jc * 128:
                                  8192 + i * 1024 + jc * 128 + 128],
                            ar.ba[:, jc * 512 + i * 128: jc * 512 + i * 128 + 128],
                            tdeps, k % 2, prev2[k % 2])
                prev2[k % 2] = tev
                ev2.append(tev)
        pv["trexp"] = ev2[-1]
        kexp_prev = None
        knorm_toks = []
        kexp_toks = []
        for i in range(4):
            kmm = []
            for h in range(2):
                pairs = [(ar.bb[:, jk * 512 + i * 128: jk * 512 + i * 128 + 128],
                          ar.wa[:, jk * 3072 + 1024 + h * 512:
                                jk * 3072 + 1024 + h * 512 + 512])
                         for jk in range(8)]
                pairs.append((ar.ones[0:1, 0:128],
                              ar.br[0:1, 1024 + h * 512:1024 + h * 512 + 512]))
                mdeps = list(ev)
                if kexp_prev is not None:
                    mdeps.append(kexp_prev)
                elif pv.get("gstt") is not None:
                    mdeps.append(pv["gstt"])
                kmm.append(mm_accum(
                    p, ar.pa[:, 2048 + h * 512:2048 + h * 512 + 512],
                    pairs, deps=mdeps))
            tkm = p.op("vector",
                       lambda e, i=i: e.reduce_max(
                           ar.ms[:, 88 + i:89 + i], ar.pa[:, 2048:3072],
                           axis=AX.X, negate=True),
                       deps=kmm, selfwait=False)
            kedeps = [tkm]
            if i == 0 and pv.get("gmm") is not None:
                kedeps.append(pv["gmm"])
            tke = p.op("scalar",
                       lambda e, i=i: e.activation(
                           ar.ba[:, 4096 + i * 1024:4096 + (i + 1) * 1024],
                           ar.pa[:, 2048:3072], AF.Exp,
                           bias=ar.ms[:, 88 + i:89 + i],
                           accum_out=ar.ms[:, 92 + i:93 + i]),
                       deps=kedeps, selfwait=False)
            kexp_prev = tke
            kexp_toks.append(tke)
        trc = p.op("vector",
                   lambda e: e.reciprocal(ar.ms[:, 96:100], ar.ms[:, 92:96]),
                   deps=kexp_toks, selfwait=True)
        for i in range(4):
            knorm_toks.append(p.op(
                "vector",
                lambda e, i=i: e.tensor_scalar(
                    ar.ba[:, 4096 + i * 1024:4096 + (i + 1) * 1024],
                    ar.ba[:, 4096 + i * 1024:4096 + (i + 1) * 1024],
                    ar.ms[:, 96 + i:97 + i], None, ALU.mult),
                deps=[trc], selfwait=True))
        vev = []
        for jc in range(8):
            jcl = jc % 4
            pairs = [(ar.wa[:, jk * 3072 + 2048 + jc * 128:
                           jk * 3072 + 2048 + jc * 128 + 128],
                      ar.bb[:, jk * 512:(jk + 1) * 512]) for jk in range(8)]
            pairs.append((ar.br[0:1, 2048 + jc * 128:2048 + jc * 128 + 128],
                          ar.ones[0:1, 0:512]))
            mdeps = [exp_toks[jcl], exp_toks[4 + jcl]]
            if jc >= 4:
                mdeps.append(vev[jc - 4])
            tmm = mm_accum(p, ar.pa[:, jcl * 512:(jcl + 1) * 512], pairs,
                           deps=mdeps)
            vdeps = [tmm]
            if jc == 0 and pv.get("vdma") is not None:
                vdeps.append(pv["vdma"])
            vev.append(p.op("scalar",
                            lambda e, jc=jc, jcl=jcl: e.copy(
                                ar.ba[:, 12288 + jc * 512:12288 + (jc + 1) * 512],
                                ar.pa[:, jcl * 512:(jcl + 1) * 512]),
                            deps=vdeps, selfwait=False))
        pv["vev"] = [vev[4], vev[5], vev[6], vev[7]]
        pv["vdma"] = p.dma("sync", fm_block_ap(dr["s_v"], 0, 8, tt, 512),
                           sb3(ar.ba[:, 12288:16384], 512), deps=vev)
        for m in range(8):
            for h in range(2):
                slot = (m * 2 + h) % 2
                pairs = [(ar.ba[:, 8192 + i * 1024 + m * 128:
                               8192 + i * 1024 + m * 128 + 128],
                          ar.ba[:, 4096 + i * 1024 + h * 512:
                                4096 + i * 1024 + h * 512 + 512])
                         for i in range(4)]
                mdeps = ev2 + knorm_toks + [kexp_toks[3]]
                if gprev[slot] is not None:
                    mdeps.append(gprev[slot])
                tmm = mm_accum(p, ar.pa[:, 2048 + slot * 512:2048 + slot * 512 + 512],
                               pairs, deps=mdeps)
                tst = p.op("vector",
                           lambda e, m=m, h=h, slot=slot:
                           e.scalar_tensor_tensor(
                               ar.fa[:, m * 1024 + h * 512:
                                     m * 1024 + h * 512 + 512],
                               ar.fa[:, m * 1024 + h * 512:
                                     m * 1024 + h * 512 + 512],
                               ar.ms[:, 56 + m:57 + m],
                               ar.pa[:, 2048 + slot * 512:2048 + slot * 512 + 512],
                               ALU.mult, ALU.add),
                           deps=[tmm, sj_tok0, sj_tok1], selfwait=True)
                gprev[slot] = tst
        pv["gmm"] = tmm
        pv["gstt"] = gprev[slot]
        first = False
    p.barrier()


def phase_att_pass2(p, ar, dr, t0, t1):
    oprev = [None, None]
    aprev = [None, None]
    pv = {}
    for ck in range((t1 - t0) // 512):
        tt = t0 + ck * 512
        par = ck % 2
        vb = par * 4096
        ob = 8192 + par * 4096
        xt = ar.fb[:, 0:4096] if par == 0 else ar.fa[:, 0:4096]
        tV = p.dma("sync", sb3(ar.ba[:, vb:vb + 4096], 512),
                   fm_block_ap(dr["s_v"], 0, 8, tt, 512),
                   deps=[pv.get(("v", par))])
        tX = p.dma("sync", sb3(xt, D), stream_chunk_ap(dr["s_att"], tt),
                   deps=[pv.get(("x", par))])
        ofm = []
        for m in range(8):
            slot = m % 2
            pairs = [(ar.bb[:, n * 1024 + m * 128: n * 1024 + m * 128 + 128],
                      ar.ba[:, vb + n * 512: vb + (n + 1) * 512])
                     for n in range(8)]
            mdeps = [tV]
            if oprev[slot] is not None:
                mdeps.append(oprev[slot])
            tmm = mm_accum(p, ar.pa[:, slot * 512:slot * 512 + 512], pairs,
                           deps=mdeps)
            edeps = [tmm]
            if m < 2 and pv.get(("o", par)) is not None:
                edeps.append(pv[("o", par)])
            tev = p.op("scalar",
                       lambda e, m=m, slot=slot, ob=ob: e.copy(
                           ar.ba[:, ob + m * 512: ob + (m + 1) * 512],
                           ar.pa[:, slot * 512:slot * 512 + 512]),
                       deps=edeps, selfwait=False)
            oprev[slot] = tev
            ofm.append(tev)
        adds = []
        for i in range(4):
            for nh in range(2):
                slot = 2 + (i * 2 + nh) % 2
                pairs = [(ar.ba[:, ob + m * 512 + i * 128:
                               ob + m * 512 + i * 128 + 128],
                          ar.wa[:, 24576 + m * 1024 + nh * 512:
                                24576 + m * 1024 + nh * 512 + 512])
                         for m in range(8)]
                pairs.append((ar.ones[0:1, 0:128],
                              ar.br[0:1, 3072 + nh * 512:3072 + nh * 512 + 512]))
                mdeps = ofm + [tX]
                if aprev[slot - 2] is not None:
                    mdeps.append(aprev[slot - 2])
                tmm = mm_accum(p, ar.pa[:, slot * 512:slot * 512 + 512], pairs,
                               deps=mdeps)
                tad = p.op("vector",
                           lambda e, i=i, nh=nh, slot=slot, xt=xt:
                           e.tensor_add(
                               xt[:, i * 1024 + nh * 512:
                                  i * 1024 + nh * 512 + 512],
                               xt[:, i * 1024 + nh * 512:
                                  i * 1024 + nh * 512 + 512],
                               ar.pa[:, slot * 512:slot * 512 + 512]),
                           deps=[tmm], selfwait=True)
                aprev[slot - 2] = tad
                adds.append(tad)
        pv[("o", par)] = tmm
        pv[("x", par)] = p.dma("scalar", stream_chunk_ap(dr["s_att"], tt),
                               sb3(xt, D), deps=adds)
        pv[("v", par)] = tmm
    p.barrier()


def phase_att_gfin(p, ar, dr):
    trc = p.op("vector",
               lambda e: e.reciprocal(ar.fs[:, 24:32], ar.fs[:, 16:24]),
               selfwait=True)
    casts = []
    for m in range(8):
        casts.append(p.op("vector",
                          lambda e, m=m: e.tensor_scalar(
                              ar.ba[:, m * 1024:(m + 1) * 1024],
                              ar.fa[:, m * 1024:(m + 1) * 1024],
                              ar.fs[:, 24 + m:25 + m], None, ALU.mult),
                          deps=[trc], selfwait=True))
    prev = [None, None]
    for m in range(8):
        for n in range(8):
            k = m * 8 + n
            tev = tr128(p, ar,
                        ar.bb[:, n * 1024 + m * 128: n * 1024 + m * 128 + 128],
                        ar.ba[:, m * 1024 + n * 128: m * 1024 + n * 128 + 128],
                        [casts[m]], k % 2, prev[k % 2])
            prev[k % 2] = tev
    p.barrier()


def phase_att_pass2(p, ar, dr, t0, t1):
    for ck in range((t1 - t0) // 512):
        tt = t0 + ck * 512
        tV = p.dma("sync", sb3(ar.ba[:, 0:4096], 512),
                   fm_block_ap(dr["s_v"], 0, 8, tt, 512))
        tX = p.dma("sync", sb3(ar.fb[:, 0:4096], D),
                   stream_chunk_ap(dr["s_att"], tt))
        ofm = []
        oprev = [None, None]
        for m in range(8):
            slot = m % 2
            pairs = [(ar.bb[:, n * 1024 + m * 128: n * 1024 + m * 128 + 128],
                      ar.ba[:, n * 512:(n + 1) * 512]) for n in range(8)]
            mdeps = [tV]
            if oprev[slot] is not None:
                mdeps.append(oprev[slot])
            tmm = mm_accum(p, ar.pa[:, slot * 512:slot * 512 + 512], pairs,
                           deps=mdeps)
            tev = p.op("scalar",
                       lambda e, m=m, slot=slot: e.copy(
                           ar.ba[:, 4096 + m * 512:4096 + (m + 1) * 512],
                           ar.pa[:, slot * 512:slot * 512 + 512]),
                       deps=[tmm], selfwait=False)
            oprev[slot] = tev
            ofm.append(tev)
        adds = []
        aprev = [None, None]
        for i in range(4):
            for nh in range(2):
                slot = 2 + (i * 2 + nh) % 2
                pairs = [(ar.ba[:, 4096 + m * 512 + i * 128:
                               4096 + m * 512 + i * 128 + 128],
                          ar.wa[:, 24576 + m * 1024 + nh * 512:
                                24576 + m * 1024 + nh * 512 + 512])
                         for m in range(8)]
                pairs.append((ar.ones[0:1, 0:128],
                              ar.br[0:1, 3072 + nh * 512:3072 + nh * 512 + 512]))
                mdeps = ofm + [tX]
                if aprev[slot - 2] is not None:
                    mdeps.append(aprev[slot - 2])
                tmm = mm_accum(p, ar.pa[:, slot * 512:slot * 512 + 512], pairs,
                               deps=mdeps)
                tad = p.op("vector",
                           lambda e, i=i, nh=nh, slot=slot: e.tensor_add(
                               ar.fb[:, i * 1024 + nh * 512:
                                     i * 1024 + nh * 512 + 512],
                               ar.fb[:, i * 1024 + nh * 512:
                                     i * 1024 + nh * 512 + 512],
                               ar.pa[:, slot * 512:slot * 512 + 512]),
                           deps=[tmm], selfwait=True)
                aprev[slot - 2] = tad
                adds.append(tad)
        p.dma("scalar", stream_chunk_ap(dr["s_att"], tt),
              sb3(ar.fb[:, 0:4096], D), deps=adds)
        p.barrier()


def phase_att_layer(p, ar, dr, l, t0, t1):
    phase_att_setup(p, ar, dr, l)
    phase_att_pass1(p, ar, dr, t0, t1)
    phase_att_gfin(p, ar, dr)
    phase_att_pass2(p, ar, dr, t0, t1)



def phase_make_fm(p, ar, dr, t0, t1, src="s_f", dst="s_fm"):
    for ck in range((t1 - t0) // 512):
        tt = t0 + ck * 512
        tX = p.dma("gpsimd", sb3(ar.ba[:, 0:4096], D),
                   stream_chunk_ap(dr[src], tt))
        ev = []
        prev = [None, None]
        for i in range(4):
            for j in range(8):
                k = i * 8 + j
                tev = tr128(p, ar,
                            ar.bb[:, j * 512 + i * 128: j * 512 + i * 128 + 128],
                            ar.ba[:, i * 1024 + j * 128: i * 1024 + j * 128 + 128],
                            [tX], k % 2, prev[k % 2])
                prev[k % 2] = tev
                ev.append(tev)
        p.dma("sync", fm_block_ap(dr[dst], 0, 8, tt, 512),
              sb3(ar.bb[:, 0:4096], 512), deps=ev)
        p.barrier()


def phase_moe_gate(p, ar, dr, t0, t1, E, topk, gw_key, gb_key=None,
                   s_fm="s_fm", s_wd="s_wd"):
    tw = p.dma("gpsimd", sb3(ar.wa[:, 0:8 * E], E),
               dr[gw_key].rearrange("(c p) e -> p c e", p=128))
    p.op("vector", lambda e: e.memset(ar.ms[:, 104:168], 0.0))
    tgb = None
    if gb_key is not None:
        tgb = p.dma("gpsimd", ar.br[0:1, 0:E], row2d(dr[gb_key]))
    p.barrier()
    for ck in range((t1 - t0) // 512):
        tt = t0 + ck * 512
        tX = p.dma("sync", sb3(ar.bb[:, 0:4096], 512),
                   fm_block_ap(dr[s_fm], 0, 8, tt, 512))
        wtoks = []
        eprev = None
        for i in range(4):
            pairs = [(ar.bb[:, jk * 512 + i * 128: jk * 512 + i * 128 + 128],
                      ar.wa[:, jk * E:(jk + 1) * E]) for jk in range(8)]
            if gb_key is not None:
                pairs.append((ar.ones[0:1, 0:128], ar.br[0:1, 0:E]))
            mdeps = [tX]
            if eprev is not None:
                mdeps.append(eprev)
            tmm = mm_accum(p, ar.pa[:, 0:E], pairs, deps=mdeps)
            c = 0
            tmx = p.op("vector",
                       lambda e, i=i: e.reduce_max(
                           ar.ms[:, 0:1], ar.pa[:, 0:E], axis=AX.X,
                           negate=True),
                       deps=[tmm], selfwait=True)
            tex = p.op("scalar",
                       lambda e, i=i: e.activation(
                           ar.ms[:, 8:8 + E], ar.pa[:, 0:E], AF.Exp,
                           bias=ar.ms[:, 0:1], accum_out=ar.ms[:, 1:2]),
                       deps=[tmx], selfwait=True)
            eprev = tex
            trc = p.op("vector",
                       lambda e: e.reciprocal(ar.ms[:, 2:3], ar.ms[:, 1:2]),
                       deps=[tex], selfwait=True)
            tsm = p.op("vector",
                       lambda e: e.tensor_scalar(
                           ar.ms[:, 8:8 + E], ar.ms[:, 8:8 + E],
                           ar.ms[:, 2:3], None, ALU.mult),
                       deps=[trc], selfwait=True)
            tprev = tsm
            tcur = p.op("vector",
                        lambda e: e.tensor_copy(ar.ms[:, 32:32 + E],
                                                ar.ms[:, 8:8 + E]),
                        deps=[tsm], selfwait=True)
            for r in range(topk - 1):
                tmx2 = p.op("vector",
                            lambda e: e.reduce_max(ar.ms[:, 3:4],
                                                   ar.ms[:, 32:32 + E],
                                                   axis=AX.X),
                            deps=[tcur], selfwait=True)
                tmsk = p.op("vector",
                            lambda e: e.tensor_scalar(
                                ar.ms[:, 56:56 + E], ar.ms[:, 32:32 + E],
                                ar.ms[:, 3:4], None, ALU.is_equal),
                            deps=[tmx2], selfwait=True)
                tcur = p.op("vector",
                            lambda e: e.scalar_tensor_tensor(
                                ar.ms[:, 32:32 + E], ar.ms[:, 56:56 + E],
                                -1e30, ar.ms[:, 32:32 + E],
                                ALU.mult, ALU.add),
                            deps=[tmsk], selfwait=True)
            tthr = p.op("vector",
                        lambda e: e.reduce_max(ar.ms[:, 4:5],
                                               ar.ms[:, 32:32 + E],
                                               axis=AX.X),
                        deps=[tcur], selfwait=True)
            tm2 = p.op("vector",
                       lambda e: e.tensor_scalar(
                           ar.ms[:, 56:56 + E], ar.ms[:, 8:8 + E],
                           ar.ms[:, 4:5], None, ALU.is_ge),
                       deps=[tthr], selfwait=True)
            twd = p.op("vector",
                       lambda e, i=i: e.tensor_mul(
                           ar.ms[:, 104 + i * 16:104 + i * 16 + E],
                           ar.ms[:, 8:8 + E], ar.ms[:, 56:56 + E]),
                       deps=[tm2], selfwait=True)
            wtoks.append(twd)
        pvfm["eprev"] = eprev
        pvfm["wdst"] = p.dma(
            "scalar",
            dr[s_wd][tt:tt + 512, 0:16].rearrange("(n p) e -> p n e", p=128),
            sb3(ar.ms[:, 104:168], 16), deps=wtoks)
    p.barrier()


def phase_moe_expert(p, ar, dr, t0, t1, *, w1_ap, w2_ap, hdim, gated,
                     b1_ap=None, b2_ap=None, wd_col=None, z_first=False,
                     s_fm="s_fm", s_z="s_z"):
    actdim = hdim // 2 if gated else hdim
    nh = hdim // 128
    na = actdim // 128
    w1toks = load_wT_bf16(p, ar.wa, w1_ap, D, hdim, col0=0)
    w2toks = load_wT_bf16(p, ar.wa, w2_ap, actdim, D, col0=8 * hdim)
    brow_toks = []
    if b1_ap is not None:
        brow_toks.append(p.dma("gpsimd", ar.br[0:1, 0:hdim], row2d(b1_ap)))
    if b2_ap is not None:
        brow_toks.append(p.dma("gpsimd", ar.br[0:1, 2048:3072], row2d(b2_ap)))
    xfm_last = [None, None]
    wd_last = [None, None]
    eo_last = [None, None]
    nck = (t1 - t0) // 512
    for ck in range(nck):
        tt = t0 + ck * 512
        par = ck % 2
        xb = 4096 * par
        tX = p.dma("sync", sb3(ar.bb[:, xb:xb + 4096], 512),
                   fm_block_ap(dr[s_fm], 0, 8, tt, 512),
                   deps=[xfm_last[par]])
        if wd_col is not None:
            wb = 104 + 64 * par
            tW = p.dma("sync", sb3(ar.ms[:, wb:wb + 64], 16),
                       dr["s_wd"][tt:tt + 512, 0:16].rearrange(
                           "(n p) e -> p n e", p=128),
                       deps=[wd_last[par]])
        hev = [None] * nh
        hmm_last = None
        for j in range(nh):
            slot = j % 2
            pairs = [(ar.wa[:, jk * hdim + j * 128: jk * hdim + j * 128 + 128],
                      ar.bb[:, xb + jk * 512: xb + (jk + 1) * 512])
                     for jk in range(8)]
            if b1_ap is not None:
                pairs.append((ar.br[0:1, j * 128:j * 128 + 128],
                              ar.ones[0:1, 0:512]))
            mdeps = [tX] + w1toks + brow_toks
            if j >= 2:
                mdeps.append(hev[j - 2])
            tmm = mm_accum(p, ar.pa[:, slot * 512:slot * 512 + 512], pairs,
                           deps=mdeps)
            hmm_last = tmm
            if gated:
                if j < nh // 2:
                    tev = p.op("scalar",
                               lambda e, j=j, slot=slot: e.copy(
                                   ar.ba[:, j * 512:(j + 1) * 512],
                                   ar.pa[:, slot * 512:slot * 512 + 512]),
                               deps=[tmm], selfwait=False)
                else:
                    jj = j - nh // 2
                    tev = p.op("scalar",
                               lambda e, jj=jj, slot=slot: e.activation(
                                   ar.ba[:, 4096 + jj * 512:
                                         4096 + (jj + 1) * 512],
                                   ar.pa[:, slot * 512:slot * 512 + 512],
                                   AF.Silu),
                               deps=[tmm], selfwait=False)
            else:
                tev = p.op("scalar",
                           lambda e, j=j, slot=slot: e.activation(
                               ar.ba[:, 8192 + j * 512:8192 + (j + 1) * 512],
                               ar.pa[:, slot * 512:slot * 512 + 512],
                               AF.Gelu),
                           deps=[tmm], selfwait=False)
            hev[j] = tev
        xfm_last[par] = hmm_last
        if gated:
            atoks = []
            for j in range(na):
                atoks.append(p.op(
                    "vector",
                    lambda e, j=j: e.tensor_mul(
                        ar.ba[:, 8192 + j * 512:8192 + (j + 1) * 512],
                        ar.ba[:, j * 512:(j + 1) * 512],
                        ar.ba[:, 4096 + j * 512:4096 + (j + 1) * 512]),
                    deps=[hev[j], hev[j + nh // 2]], selfwait=False))
        else:
            atoks = hev
        eost = ar.fb if par == 0 else ar.fa
        etoks = []
        eprev = [None, None]
        for i in range(4):
            for nhf in range(2):
                slot = 2 + (i * 2 + nhf) % 2
                pairs = [(ar.ba[:, 8192 + m * 512 + i * 128:
                               8192 + m * 512 + i * 128 + 128],
                          ar.wa[:, 8 * hdim + m * 1024 + nhf * 512:
                                8 * hdim + m * 1024 + nhf * 512 + 512])
                         for m in range(na)]
                if b2_ap is not None:
                    pairs.append((ar.ones[0:1, 0:128],
                                  ar.br[0:1, 2048 + nhf * 512:
                                        2048 + nhf * 512 + 512]))
                mdeps = list(atoks) + w2toks
                if eprev[slot - 2] is not None:
                    mdeps.append(eprev[slot - 2])
                tmm = mm_accum(p, ar.pa[:, slot * 512:slot * 512 + 512],
                               pairs, deps=mdeps)
                vdeps = [tmm, eo_last[par]]
                if wd_col is not None:
                    vdeps.append(tW)
                    tev = p.op("vector",
                               lambda e, i=i, nhf=nhf, slot=slot, wb=wb,
                               eost=eost:
                               e.tensor_scalar(
                                   eost[:, i * 1024 + nhf * 512:
                                        i * 1024 + nhf * 512 + 512],
                                   ar.pa[:, slot * 512:slot * 512 + 512],
                                   ar.ms[:, wb + i * 16 + wd_col:
                                         wb + i * 16 + wd_col + 1],
                                   None, ALU.mult),
                               deps=vdeps, selfwait=False)
                else:
                    tev = p.op("vector",
                               lambda e, i=i, nhf=nhf, slot=slot, eost=eost:
                               e.tensor_copy(
                                   eost[:, i * 1024 + nhf * 512:
                                        i * 1024 + nhf * 512 + 512],
                                   ar.pa[:, slot * 512:slot * 512 + 512]),
                               deps=vdeps, selfwait=False)
                eprev[slot - 2] = tev
                etoks.append(tev)
        tz = p.dma("gpsimd", stream_chunk_ap(dr[s_z], tt),
                   sb3(eost[:, 0:4096], D),
                   deps=etoks, accum=not z_first)
        eo_last[par] = tz
        if wd_col is not None:
            wd_last[par] = etoks[-1]
    p.barrier()


def phase_copy_stream(p, ar, dr, t0, t1, src, dst):
    prev = None
    for ck in range((t1 - t0) // 512):
        tt = t0 + ck * 512
        half = (ck % 2) * 4096
        t = p.dma("sync", sb3(ar.fa[:, half:half + 4096], D),
                  stream_chunk_ap(dr[src], tt),
                  deps=[prev] if prev else ())
        prev = p.dma("scalar", stream_chunk_ap(dr[dst], tt),
                     sb3(ar.fa[:, half:half + 4096], D), deps=[t])
    p.barrier()



def phase_ffn0(p, ar, dr, t0, t1):
    phase_make_fm_gate(p, ar, dr, t0, t1, E=8, topk=2, gw_key="gate_wT")
    for e in range(8):
        phase_moe_expert(
            p, ar, dr, t0, t1,
            w1_ap=dr["W1Tf"][e * 1024:(e + 1) * 1024, :],
            w2_ap=dr["W2Tf"][e * 768:(e + 1) * 768, :],
            hdim=1536, gated=True, wd_col=e, z_first=(e == 0))
    for s in range(2):
        phase_moe_expert(
            p, ar, dr, t0, t1,
            w1_ap=dr["Ws1P"][:, s * 1536:(s + 1) * 1536],
            w2_ap=dr["Ws2P"][s * 768:(s + 1) * 768, :],
            hdim=1536, gated=True, wd_col=None, z_first=False)



def phase_ffn1(p, ar, dr, t0, t1):
    phase_make_fm_gate(p, ar, dr, t0, t1, E=15, topk=3, gw_key="gWT",
                       gb_key="gb")
    phase_copy_stream(p, ar, dr, t0, t1, "s_f", "s_z")
    for s in range(4):
        phase_moe_expert(
            p, ar, dr, t0, t1,
            w1_ap=dr["sW1m"][:, s * 1024:(s + 1) * 1024],
            w2_ap=dr["sW2m"][s * 1024:(s + 1) * 1024, :],
            hdim=1024, gated=False,
            b1_ap=dr["sb1"][s * 1024:(s + 1) * 1024],
            b2_ap=dr["sb2"] if s == 0 else None,
            wd_col=None, z_first=False)
    for e in range(15):
        phase_moe_expert(
            p, ar, dr, t0, t1,
            w1_ap=dr["rW1f"][e * 1024:(e + 1) * 1024, :],
            w2_ap=dr["rW2f"][e * 2048:(e + 1) * 2048, :],
            hdim=2048, gated=False,
            b1_ap=dr["rb1f"][e * 2048:(e + 1) * 2048],
            b2_ap=dr["rb2f"][e * 1024:(e + 1) * 1024],
            wd_col=e, z_first=False)


def tr128f(p, ar, dst, src, deps, slot, prev_evac, evac="vector"):
    tdeps = list(deps)
    if prev_evac is not None:
        tdeps.append(prev_evac)
    t1 = p.op("tensor",
              lambda e, dst=dst, src=src, slot=slot:
              e.transpose(ar.pa[:, 2048 + slot * 512:2048 + slot * 512 + 128],
                          src, ar.eyef[:, :]),
              deps=tdeps, selfwait=False)
    t2 = p.op(evac,
              lambda e, dst=dst, slot=slot:
              e.tensor_copy(dst, ar.pa[:, 2048 + slot * 512:2048 + slot * 512 + 128]),
              deps=[t1], selfwait=False)
    return t2


def phase_make_fm_gate(p, ar, dr, t0, t1, E=None, topk=None, gw_key=None,
                       gb_key=None, src="s_f", dst="s_fm", s_wd="s_wd"):
    if E is not None:
        tw = p.dma("sync", sb3(ar.fs[:, 256:256 + 8 * 16], 16)[:, :, 0:E],
                   dr[gw_key].rearrange("(c p) e -> p c e", p=128))
        if gb_key is not None:
            p.dma("sync", ar.brf[0:1, 0:E], row2d(dr[gb_key]))
        p.op("vector", lambda e: e.memset(ar.ms[:, 104:168], 0.0))
        p.op("vector", lambda e: e.memset(ar.fs[0:1, 384:512], 1.0))
        p.barrier()
    pvfm = {}
    for ck in range((t1 - t0) // 512):
        tt = t0 + ck * 512
        par = ck % 2
        ab = par * 4096
        tX = p.dma("gpsimd", sb3(ar.ba[:, ab:ab + 4096], D),
                   stream_chunk_ap(dr[src], tt),
                   deps=[pvfm.get(("tr", par))])
        tXf = p.dma("sync", sb3(ar.fa[:, ab:ab + 4096], D),
                    stream_chunk_ap(dr[src], tt),
                    deps=[pvfm.get(("trf", par))]) if E is not None else None
        ev = []
        prev = [None, None]
        for i in range(4):
            for j in range(8):
                k = i * 8 + j
                tdeps = [tX]
                if k < 2 and pvfm.get(("fmdma", par)) is not None:
                    tdeps.append(pvfm[("fmdma", par)])
                tev = tr128(p, ar,
                            ar.bb[:, ab + j * 512 + i * 128:
                                  ab + j * 512 + i * 128 + 128],
                            ar.ba[:, ab + i * 1024 + j * 128:
                                  ab + i * 1024 + j * 128 + 128],
                            tdeps, k % 2, prev[k % 2])
                prev[k % 2] = tev
                ev.append(tev)
        pvfm[("tr", par)] = ev[-1]
        pvfm[("fmdma", par)] = p.dma(
            "sync", fm_block_ap(dr[dst], 0, 8, tt, 512),
            sb3(ar.bb[:, ab:ab + 4096], 512), deps=ev)
        if E is None:
            continue
        evf = []
        prevf = [None, None]
        for i in range(4):
            for j in range(8):
                k = i * 8 + j
                tdeps = [tXf]
                if k < 2 and pvfm.get("gmm") is not None:
                    tdeps.append(pvfm["gmm"])
                tev = tr128f(p, ar,
                             ar.fb[:, j * 512 + i * 128: j * 512 + i * 128 + 128],
                             ar.fa[:, ab + i * 1024 + j * 128:
                                   ab + i * 1024 + j * 128 + 128],
                             tdeps, k % 2, prevf[k % 2])
                prevf[k % 2] = tev
                evf.append(tev)
        pvfm[("trf", par)] = evf[-1]
        wtoks = []
        eprev = pvfm.get("eprev")
        for i in range(4):
            pairs = [(ar.fb[:, jk * 512 + i * 128: jk * 512 + i * 128 + 128],
                      ar.fs[:, 256 + jk * 16:256 + jk * 16 + E])
                     for jk in range(8)]
            if gb_key is not None:
                pairs.append((ar.fs[0:1, 384:512], ar.brf[0:1, 0:E]))
            mdeps = list(evf)
            if eprev is not None:
                mdeps.append(eprev)
            tmm = mm_accum(p, ar.pa[:, 0:E], pairs, deps=mdeps)
            pvfm["gmm"] = tmm
            tmx = p.op("vector",
                       lambda e: e.reduce_max(ar.ms[:, 0:1], ar.pa[:, 0:E],
                                              axis=AX.X, negate=True),
                       deps=[tmm] + ([pvfm.get("wdst")] if i == 0 else []),
                       selfwait=True)
            tex = p.op("scalar",
                       lambda e: e.activation(ar.ms[:, 8:8 + E],
                                              ar.pa[:, 0:E], AF.Exp,
                                              bias=ar.ms[:, 0:1],
                                              accum_out=ar.ms[:, 1:2]),
                       deps=[tmx], selfwait=True)
            eprev = tex
            trc = p.op("vector",
                       lambda e: e.reciprocal(ar.ms[:, 2:3], ar.ms[:, 1:2]),
                       deps=[tex], selfwait=True)
            tsm = p.op("vector",
                       lambda e: e.tensor_scalar(ar.ms[:, 8:8 + E],
                                                 ar.ms[:, 8:8 + E],
                                                 ar.ms[:, 2:3], None, ALU.mult),
                       deps=[trc], selfwait=True)
            tcur = p.op("vector",
                        lambda e: e.tensor_copy(ar.ms[:, 32:32 + E],
                                                ar.ms[:, 8:8 + E]),
                        deps=[tsm], selfwait=True)
            for r in range(topk - 1):
                tmx2 = p.op("vector",
                            lambda e: e.reduce_max(ar.ms[:, 3:4],
                                                   ar.ms[:, 32:32 + E],
                                                   axis=AX.X),
                            deps=[tcur], selfwait=True)
                tmsk = p.op("vector",
                            lambda e: e.tensor_scalar(
                                ar.ms[:, 56:56 + E], ar.ms[:, 32:32 + E],
                                ar.ms[:, 3:4], None, ALU.is_equal),
                            deps=[tmx2], selfwait=True)
                tcur = p.op("vector",
                            lambda e: e.scalar_tensor_tensor(
                                ar.ms[:, 32:32 + E], ar.ms[:, 56:56 + E],
                                -1e30, ar.ms[:, 32:32 + E],
                                ALU.mult, ALU.add),
                            deps=[tmsk], selfwait=True)
            tthr = p.op("vector",
                        lambda e: e.reduce_max(ar.ms[:, 4:5],
                                               ar.ms[:, 32:32 + E], axis=AX.X),
                        deps=[tcur], selfwait=True)
            tm2 = p.op("vector",
                       lambda e: e.tensor_scalar(ar.ms[:, 56:56 + E],
                                                 ar.ms[:, 8:8 + E],
                                                 ar.ms[:, 4:5], None, ALU.is_ge),
                       deps=[tthr], selfwait=True)
            twd = p.op("vector",
                       lambda e, i=i: e.tensor_mul(
                           ar.ms[:, 104 + i * 16:104 + i * 16 + E],
                           ar.ms[:, 8:8 + E], ar.ms[:, 56:56 + E]),
                       deps=[tm2], selfwait=True)
            wtoks.append(twd)
        pvfm["eprev"] = eprev
        pvfm["wdst"] = p.dma(
            "scalar",
            dr[s_wd][tt:tt + 512, 0:16].rearrange("(n p) e -> p n e", p=128),
            sb3(ar.ms[:, 104:168], 16), deps=wtoks)
    p.barrier()



def phase_conv(p, ar, dr, t0, t1):
    tw = load_wT_bf16(p, ar.wa, dr["x_projT"], 4096, 96, col0=0)
    tcw = p.dma("sync", ar.fs[:, 0:128], dr["conv_wc"])
    tcb = p.dma("sync", ar.fs[:, 128:160], dr["conv_bc"])
    p.barrier()
    nt = (t1 - t0) // 2048
    for ck in range(nt):
        tt = t0 + ck * 2048
        xev = [None, None, None, None]
        prev_t3 = None
        prev_d1 = None
        prev_mm = []
        prev_d2 = None
        for j in range(32):
            fdeps = [prev_t3] if prev_t3 else []
            if tt == 0:
                tz = p.op("vector",
                          lambda e, j=j: e.memset(ar.fa[:, 0:3], 0.0),
                          deps=fdeps, selfwait=False)
                tl = p.dma("sync", ar.fa[:, 3:2051],
                           dr["s_uraw"][j * 128:(j + 1) * 128, tt:tt + 2048],
                           deps=fdeps + [tz])
                ldeps = [tl, tz]
            else:
                tl = p.dma("sync", ar.fa[:, 0:2051],
                           dr["s_uraw"][j * 128:(j + 1) * 128, tt - 3:tt + 2048],
                           deps=fdeps)
                ldeps = [tl]
                if tt == T_HALF and "cmask128" in dr:
                    tmsk = p.op("vector",
                                lambda e: e.tensor_scalar(
                                    ar.fa[:, 0:3], ar.fa[:, 0:3],
                                    ar.cf[:, 2:3], None, ALU.mult),
                                deps=[tl], selfwait=False)
                    ldeps = [tl, tmsk]
            t0_ = p.op("vector",
                       lambda e, j=j: e.tensor_scalar(
                           ar.fb[:, 0:2048], ar.fa[:, 0:2048],
                           ar.fs[:, j * 4:j * 4 + 1],
                           ar.fs[:, 128 + j:129 + j],
                           ALU.mult, ALU.add),
                       deps=ldeps, selfwait=False)
            t1_ = p.op("vector",
                       lambda e, j=j: e.scalar_tensor_tensor(
                           ar.fb[:, 2048:4096], ar.fa[:, 1:2049],
                           ar.fs[:, j * 4 + 1:j * 4 + 2],
                           ar.fb[:, 0:2048], ALU.mult, ALU.add),
                       deps=[t0_], selfwait=True)
            t2_ = p.op("vector",
                       lambda e, j=j: e.scalar_tensor_tensor(
                           ar.fb[:, 0:2048], ar.fa[:, 2:2050],
                           ar.fs[:, j * 4 + 2:j * 4 + 3],
                           ar.fb[:, 2048:4096], ALU.mult, ALU.add),
                       deps=[t1_], selfwait=True)
            t3_ = p.op("vector",
                       lambda e, j=j: e.scalar_tensor_tensor(
                           ar.fb[:, 2048:4096], ar.fa[:, 3:2051],
                           ar.fs[:, j * 4 + 3:j * 4 + 4],
                           ar.fb[:, 0:2048], ALU.mult, ALU.add),
                       deps=[t2_], selfwait=True)
            tsil = p.op("scalar",
                        lambda e: e.activation(ar.fa[:, 2176:4224],
                                               ar.fb[:, 2048:4096], AF.Silu),
                        deps=[t3_] + ([prev_d1] if prev_d1 else []),
                        selfwait=False)
            tcast = p.op("vector",
                         lambda e: e.tensor_copy(ar.bb[:, 0:2048],
                                                 ar.fa[:, 2176:4224]),
                         deps=[tsil] + prev_mm +
                         ([prev_d2] if prev_d2 else []), selfwait=True)
            d1 = p.dma("scalar", dr["s_uc"][j * 128:(j + 1) * 128, tt:tt + 2048],
                       ar.fa[:, 2176:4224], deps=[tsil])
            d2 = p.dma("gpsimd", dr["s_ucb"][j * 128:(j + 1) * 128, tt:tt + 2048],
                       ar.bb[:, 0:2048], deps=[tcast])
            prev_mm = []
            for i in range(4):
                tmmm = p.op(
                    "tensor",
                    lambda e, j=j, i=i: e.matmul(
                        ar.pa[0:96, i * 512:(i + 1) * 512],
                        ar.wa[:, j * 96:(j + 1) * 96],
                        ar.bb[:, i * 512:(i + 1) * 512],
                        start=(j == 0), stop=(j == 31)),
                    deps=[tcast], selfwait=False)
                xev[i] = tmmm
                prev_mm.append(tmmm)
            prev_t3 = t3_
            prev_d1 = d1
            prev_d2 = d2
        etoks = []
        for i in range(4):
            etoks.append(p.op("vector",
                              lambda e, i=i: e.tensor_copy(
                                  ar.fs[0:96, 1024 + i * 512:1024 + (i + 1) * 512],
                                  ar.pa[0:96, i * 512:(i + 1) * 512]),
                              deps=[xev[i]], selfwait=False))
        p.dma("sync", dr["s_xdbl"][0:96, tt:tt + 2048],
              ar.fs[0:96, 1024:3072], deps=etoks)
        p.barrier()



def phase_dt(p, ar, dr, t0, t1):
    tw = p.dma("gpsimd", ar.wa[0:64, 0:4096], dr["dt_projT"])
    tb = p.dma("gpsimd", ar.br[0:1, 0:4096], row2d(dr["dt_proj_b"]))
    p.barrier()
    nt = (t1 - t0) // 2048
    UB = [0, 4096]
    DB = [2048, 6144]
    SB = [0, 2048]
    last_nd = [None, None]
    last_d1 = [None, None]
    last_d2 = [None, None]
    last_sg = [None, None]
    prev_ln = [None, None]
    for ck in range(nt):
        tt = t0 + ck * 2048
        tdi = p.dma("gpsimd", ar.bb[0:64, 2048 * (ck % 2):2048 * (ck % 2) + 2048],
                    dr["s_xdbl"][0:64, tt:tt + 2048],
                    deps=[last_nd[0], last_nd[1]] if ck else ())
        xb = 2048 * (ck % 2)
        for j in range(32):
            par = j % 2
            tu = p.dma("sync", ar.fa[:, UB[par]:UB[par] + 2048],
                       dr["s_uc"][j * 128:(j + 1) * 128, tt:tt + 2048],
                       deps=[last_nd[par]])
            mms = []
            for i in range(4):
                slot = i % 2
                pairs = [(ar.wa[0:64, j * 128:(j + 1) * 128],
                          ar.bb[0:64, xb + i * 512: xb + (i + 1) * 512]),
                         (ar.br[0:1, j * 128:j * 128 + 128],
                          ar.ones[0:1, 0:512])]
                mdeps = [tdi, tw, tb]
                if len(mms) >= 2:
                    mdeps.append(mms[i - 2])
                elif last_sg[slot] is not None:
                    mdeps.append(last_sg[slot])
                tmm = mm_accum(p, ar.pa[:, slot * 512:slot * 512 + 512], pairs,
                               deps=mdeps)
                tsg = p.op("scalar",
                           lambda e, i=i, slot=slot, par=par: e.activation(
                               ar.fb[:, SB[par] + i * 512:SB[par] + (i + 1) * 512],
                               ar.pa[:, slot * 512:slot * 512 + 512],
                               AF.Sigmoid, scale=-1.0),
                           deps=[tmm] + ([last_d2[par], last_nd[par]]
                                         if i == 0 else []),
                           selfwait=False)
                mms.append(tsg)
                last_sg[slot] = tsg
            tln = [p.op("scalar",
                        lambda e, i=i, par=par: e.activation(
                            ar.fa[:, DB[par] + i * 512:DB[par] + (i + 1) * 512],
                            ar.fb[:, SB[par] + i * 512:SB[par] + (i + 1) * 512],
                            AF.Ln),
                        deps=[mms[i]] + ([last_d1[par]] if i == 0 else []),
                        selfwait=True) for i in range(4)]
            tcl = p.op("vector",
                       lambda e, par=par: e.tensor_scalar(
                           ar.fa[:, DB[par]:DB[par] + 2048],
                           ar.fa[:, DB[par]:DB[par] + 2048],
                           -87.0, None, ALU.max),
                       deps=tln, selfwait=False)
            tnd = p.op("vector",
                       lambda e, par=par: e.tensor_mul(
                           ar.fb[:, SB[par]:SB[par] + 2048],
                           ar.fa[:, DB[par]:DB[par] + 2048],
                           ar.fa[:, UB[par]:UB[par] + 2048]),
                       deps=[tcl, tu] + tln, selfwait=True)
            last_nd[par] = tnd
            last_d1[par] = p.dma("scalar",
                                 dr["s_dtm"][j * 128:(j + 1) * 128, tt:tt + 2048],
                                 ar.fa[:, DB[par]:DB[par] + 2048], deps=[tcl])
            last_d2[par] = p.dma("sync",
                                 dr["s_ndtu"][j * 128:(j + 1) * 128, tt:tt + 2048],
                                 ar.fb[:, SB[par]:SB[par] + 2048], deps=[tnd])
    p.barrier()


def phase_scan_setup(p, ar, dr):
    ta_ = p.dma("sync", ar.fs[:, 1024:1536], dr["Alogc"])
    p.op("scalar", lambda e: e.activation(ar.fs[:, 512:1024],
                                          ar.fs[:, 1024:1536], AF.Exp),
         deps=[ta_], selfwait=False)
    p.dma("gpsimd", ar.wa[0:32, 0:4096], dr["sel16"])
    p.op("vector", lambda e: e.memset(ar.fs[:, 0:512], 0.0))
    p.barrier()


def phase_scan(p, ar, dr, ta, tb, yout=True, bscale_ap=None):
    nt = (tb - ta) // 1024
    lane_tail = []
    prev_bcb = None
    for ck in range(nt):
        tt = ta + ck * 1024
        tbc = p.dma("sync", ar.fs[0:32, 1024:2048],
                    dr["s_xdbl"][64:96, tt:tt + 1024], deps=[prev_bcb])
        tbcb = p.op("vector",
                    lambda e: e.tensor_copy(ar.bb[0:32, 0:1024],
                                            ar.fs[0:32, 1024:2048]),
                    deps=[tbc] + lane_tail, selfwait=True)
        prev_bcb = tbcb
        btoks = {}
        bprev = list(lane_tail) + [None, None]
        bprev = bprev[-2:] if len(bprev) > 2 else [None, None]
        for s in range(16):
            for half in range(2):
                tB = p.op("tensor",
                          lambda e, s=s, half=half: e.matmul(
                              ar.pa[:, 2048:2560],
                              ar.wa[0:32, s * 128:(s + 1) * 128],
                              ar.bb[0:32, half * 512:half * 512 + 512],
                              start=True, stop=True),
                          deps=[tbcb, bprev[0]] + lane_tail,
                          selfwait=False)
                tC = p.op("tensor",
                          lambda e, s=s, half=half: e.matmul(
                              ar.pa[:, 2560:3072],
                              ar.wa[0:32, 2048 + s * 128:
                                    2048 + s * 128 + 128],
                              ar.bb[0:32, half * 512:half * 512 + 512],
                              start=True, stop=True),
                          deps=[bprev[1]], selfwait=False)
                eB = p.op("vector",
                          lambda e, s=s, half=half: e.tensor_scalar(
                              ar.ba[:, s * 1024 + half * 512:
                                    s * 1024 + half * 512 + 512],
                              ar.pa[:, 2048:2560],
                              -1.0 if bscale_ap is None else bscale_ap,
                              None, ALU.mult),
                          deps=[tB], selfwait=False)
                eC = p.op("vector",
                          lambda e, s=s, half=half: e.tensor_copy(
                              ar.wa[:, 4096 + s * 1024 + half * 512:
                                    4096 + s * 1024 + half * 512 + 512],
                              ar.pa[:, 2560:3072]),
                          deps=[tC], selfwait=False)
                bprev = [eB, eC]
                btoks[(s, half)] = (eB, eC)
        ball = [t for pair in btoks.values() for t in pair]
        if ck == 0:
            prev_j_v = None
            prev_j_g = None
        for jp in range(16):
            jv, jg = 2 * jp, 2 * jp + 1
            tdm_v = p.dma("sync", ar.fa[:, 0:1024],
                          dr["s_dtm"][jv * 128:(jv + 1) * 128, tt:tt + 1024],
                          deps=[prev_j_v])
            tdu_v = p.dma("scalar", ar.ms[:, 0:1024],
                          dr["s_ndtu"][jv * 128:(jv + 1) * 128, tt:tt + 1024],
                          deps=[prev_j_v])
            tdm_g = p.dma("sync", ar.fa[:, 1024:2048],
                          dr["s_dtm"][jg * 128:(jg + 1) * 128, tt:tt + 1024],
                          deps=[prev_j_g])
            tdu_g = p.dma("scalar", ar.fs[:, 2048:3072],
                          dr["s_ndtu"][jg * 128:(jg + 1) * 128, tt:tt + 1024],
                          deps=[prev_j_g])
            st_v = {"h": None, "car": None, "scan": None, "scan2": None,
                    "y": None}
            st_g = {"h": None, "car": None, "scan": None, "scan2": None,
                    "y": None}
            for s in range(16):
                for gl in (False, True):
                    st = st_g if gl else st_v
                    j = jg if gl else jv
                    col = j * 16 + s
                    lane = "gpsimd" if gl else "vector"
                    if gl:
                        da_ap = ar.fb[:, 2048:3072]
                        edeps = [tdm_g, st["scan"]]
                        b_ap = ar.fb[:, 3072:4096]
                        dut = ar.fs[:, 2048:3072]
                        dmt = ar.fa[:, 1024:2048]
                        HH, HC, YY = 5120, 6144, 7168
                        tdu_, prev_j = tdu_g, prev_j_g
                    else:
                        da_ap = ar.pa[:, (s % 2) * 1024:(s % 2) * 1024 + 1024]
                        edeps = [tdm_v, st["scan2"]]
                        b_ap = ar.fb[:, 1024:2048]
                        dut = ar.ms[:, 0:1024]
                        dmt = ar.fa[:, 0:1024]
                        HH, HC, YY = 2048, 3072, 4096
                        tdu_, prev_j = tdu_v, prev_j_v
                    tda = p.op("scalar",
                               lambda e, col=col, da_ap=da_ap, dmt=dmt:
                               e.activation(da_ap, dmt, AF.Exp,
                                            scale=ar.fs[:, 512 + col:513 + col]),
                               deps=edeps, selfwait=True)
                    tb_ = p.op(lane,
                               lambda e, s=s, b_ap=b_ap, dut=dut:
                               e.tensor_mul(b_ap, dut,
                                            ar.ba[:, s * 1024:(s + 1) * 1024]),
                               deps=[tdu_] + (ball if (jp == 0 and s == 0)
                                              else []),
                               selfwait=True)
                    tscan = p.op("vector",
                                 lambda e, col=col, da_ap=da_ap, b_ap=b_ap,
                                 HH=HH:
                                 e.tensor_tensor_scan(
                                     ar.fa[:, HH:HH + 1024], da_ap, b_ap,
                                     ar.fs[:, col:col + 1],
                                     ALU.mult, ALU.add),
                                 deps=[tb_, tda, st["h"], st["car"]],
                                 selfwait=True)
                    st["scan2"] = st["scan"]
                    st["scan"] = tscan
                    tcar = p.op("gpsimd",
                                lambda e, col=col, HH=HH: e.tensor_copy(
                                    ar.fs[:, col:col + 1],
                                    ar.fa[:, HH + 1023:HH + 1024]),
                                deps=[tscan], selfwait=True)
                    st["car"] = tcar
                    if yout:
                        if s == 0:
                            thc = p.op(lane,
                                       lambda e, s=s, HH=HH, YY=YY:
                                       e.tensor_mul(
                                           ar.fa[:, YY:YY + 1024],
                                           ar.fa[:, HH:HH + 1024],
                                           ar.wa[:, 4096 + s * 1024:
                                                 4096 + (s + 1) * 1024]),
                                       deps=[tscan] +
                                       ([prev_j] if prev_j else []),
                                       selfwait=True)
                            st["y"] = thc
                        else:
                            thc = p.op(lane,
                                       lambda e, s=s, HH=HH, HC=HC:
                                       e.tensor_mul(
                                           ar.fa[:, HC:HC + 1024],
                                           ar.fa[:, HH:HH + 1024],
                                           ar.wa[:, 4096 + s * 1024:
                                                 4096 + (s + 1) * 1024]),
                                       deps=[tscan], selfwait=True)
                            st["y"] = p.op(lane,
                                           lambda e, YY=YY, HC=HC:
                                           e.tensor_add(
                                               ar.fa[:, YY:YY + 1024],
                                               ar.fa[:, YY:YY + 1024],
                                               ar.fa[:, HC:HC + 1024]),
                                           deps=[thc, st["y"]], selfwait=True)
                        st["h"] = thc
                    else:
                        st["h"] = tscan
            if yout:
                prev_j_v = p.dma("gpsimd",
                                 dr["s_y"][jv * 128:(jv + 1) * 128, tt:tt + 1024],
                                 ar.fa[:, 4096:5120], deps=[st_v["y"]])
                prev_j_g = p.dma("gpsimd",
                                 dr["s_y"][jg * 128:(jg + 1) * 128, tt:tt + 1024],
                                 ar.fa[:, 7168:8192], deps=[st_g["y"]])
            else:
                prev_j_v = st_v["car"]
                prev_j_g = st_g["car"]
        lane_tail = [t for t in (prev_j_v, prev_j_g) if t is not None]
    p.barrier()



def phase_postscan(p, ar, dr, t0, t1):
    wtoks = load_wT_bf16(p, ar.wa, dr["out_projT"], 4096, D)
    tD = p.dma("sync", ar.fs[:, 0:32], dr["Dc"])
    p.barrier()
    grp_tail = None
    prev_mm = [None, None]
    prev_add = None
    prev_st = None
    prev_zj = [None, None]
    for ck in range((t1 - t0) // 512):
        tt = t0 + ck * 512
        tA = p.dma("sync", sb3(ar.fa[:, 0:4096], D),
                   stream_chunk_ap(dr["s_a"], tt), deps=[prev_add, prev_st])
        yf_toks = []
        for grp in range(4):
            g0 = grp * 8
            gdeps = [grp_tail]
            ty = p.dma("sync", sb3(ar.fa[:, 4096:8192], 512),
                       fm_block_ap(dr["s_y"], g0 * 128, 8, tt, 512),
                       deps=gdeps)
            tu = p.dma("scalar", sb3(ar.fb[:, 0:4096], 512),
                       fm_block_ap(dr["s_uc"], g0 * 128, 8, tt, 512),
                       deps=gdeps)
            grp_toks = []
            for jl in range(8):
                j = g0 + jl
                pz = jl % 2
                tzj = p.dma("gpsimd", ar.ms[:, pz * 512:pz * 512 + 512],
                            dr["s_mz"][j * 128:(j + 1) * 128, tt:tt + 512],
                            deps=[prev_zj[pz]])
                t1_ = p.op("vector",
                           lambda e, j=j, jl=jl, pz=pz: e.scalar_tensor_tensor(
                               ar.fs[:, 1024 + pz * 512:1024 + pz * 512 + 512],
                               ar.fb[:, jl * 512:(jl + 1) * 512],
                               ar.fs[:, j:j + 1],
                               ar.fa[:, 4096 + jl * 512:4096 + (jl + 1) * 512],
                               ALU.mult, ALU.add),
                           deps=[ty, tu] +
                           ([grp_toks[-2]] if jl >= 2 else []),
                           selfwait=True)
                t2_ = p.op("scalar",
                           lambda e, pz=pz: e.activation(
                               ar.fs[:, 2048 + pz * 512:2048 + pz * 512 + 512],
                               ar.ms[:, pz * 512:pz * 512 + 512],
                               AF.Silu),
                           deps=[tzj] + ([grp_toks[-2]] if jl >= 2 else []),
                           selfwait=False)
                bdeps = [t1_, t2_]
                if ck > 0 and grp == 0 and jl == 0:
                    bdeps.append(prev_st)
                t3_ = p.op("vector",
                           lambda e, j=j, pz=pz: e.tensor_mul(
                               ar.ba[:, j * 512:(j + 1) * 512],
                               ar.fs[:, 1024 + pz * 512:1024 + pz * 512 + 512],
                               ar.fs[:, 2048 + pz * 512:2048 + pz * 512 + 512]),
                           deps=bdeps, selfwait=True)
                prev_zj[pz] = t2_
                grp_toks.append(t3_)
            grp_tail = grp_toks[-1]
            yf_toks += grp_toks
        adds = []
        for i in range(4):
            for nh in range(2):
                slot = (i * 2 + nh) % 2
                pairs = [(ar.ba[:, m * 512 + i * 128: m * 512 + i * 128 + 128],
                          ar.wa[:, m * 1024 + nh * 512: m * 1024 + nh * 512 + 512])
                         for m in range(32)]
                mdeps = yf_toks + [tA] + wtoks
                if prev_mm[slot] is not None:
                    mdeps.append(prev_mm[slot])
                tmm = mm_accum(p, ar.pa[:, slot * 512:slot * 512 + 512], pairs,
                               deps=mdeps)
                tad = p.op("vector",
                           lambda e, i=i, nh=nh, slot=slot: e.tensor_add(
                               ar.fa[:, i * 1024 + nh * 512:
                                     i * 1024 + nh * 512 + 512],
                               ar.fa[:, i * 1024 + nh * 512:
                                     i * 1024 + nh * 512 + 512],
                               ar.pa[:, slot * 512:slot * 512 + 512]),
                           deps=[tmm], selfwait=True)
                prev_mm[slot] = tad
                adds.append(tad)
        prev_add = adds[-1]
        prev_st = p.dma("scalar", stream_chunk_ap(dr["s_a3"], tt),
                        sb3(ar.fa[:, 0:4096], D), deps=adds)
    p.barrier()



def build_core_program(h=1, limit=99, dump=None):
    import itertools
    _ctr = itertools.count()
    def _go():
        return next(_ctr) < limit
    nc = bass.Bass("TRN2", target_bir_lowering=False)
    p = Prog(nc)
    dr = {}
    ext_in = [
        ("x_in", [T_FULL, D]), ("c_eye", [128, 128]), ("c_ones", [1, 512]),
        ("w_qkvF", [6 * D, 3072]), ("w_outF", [6 * D, D]),
        ("b_outF", [6, D]), ("l0g", [128, 48]), ("l0b", [128, 48]),
        ("gate_wT", [D, 8]), ("W1Tf", [8 * D, 1536]), ("W2Tf", [8 * 768, D]),
        ("Ws1P", [D, 3072]), ("Ws2P", [1536, D]),
        ("g1c", [128, 8]), ("b1c", [128, 8]), ("w1T", [D, 4096]),
        ("bias1", [4096]), ("w2T", [4096, D]), ("bias2", [D]),
        ("ipT", [D, 8192]), ("g2c", [128, 8]), ("b2c", [128, 8]),
        ("conv_wc", [128, 128]), ("conv_bc", [128, 32]),
        ("x_projT", [4096, 96]), ("dt_projT", [64, 4096]),
        ("dt_proj_b", [4096]), ("Alogc", [128, 512]), ("sel16", [32, 4096]),
        ("Dc", [128, 32]), ("out_projT", [4096, D]),
        ("gWT", [D, 15]), ("gb", [15]), ("rW1f", [15 * D, 2048]),
        ("rb1f", [15 * 2048]), ("rW2f", [15 * 2048, D]), ("rb2f", [15 * D]),
        ("sW1m", [D, 4096]), ("sb1", [4096]), ("sW2m", [4096, D]),
        ("sb2", [D]), ("cmask128", [128, 1]), ("cmn128", [128, 1]),
    ]
    for name, shape in ext_in:
        dr[name] = p.dram(name, shape, F32, kind="ExternalInput")
    dr["out"] = p.dram("out", [T_HALF, D], F32, kind="ExternalOutput")
    for name in ("s_x", "s_att", "s_f", "s_z", "s_a", "s_a3"):
        dr[name] = p.dram(name, [T_FULL, D], F32)
    dr["s_fm"] = p.dram("s_fm", [D, T_FULL], BF16)
    dr["s_wd"] = p.dram("s_wd", [T_FULL, 16], F32)
    dr["s_h"] = p.dram("s_h", [4096, T_FULL], BF16)
    dr["s_v"] = p.dram("s_v", [D, T_FULL], BF16)
    for name in ("s_uraw", "s_mz", "s_uc", "s_dtm", "s_ndtu", "s_y"):
        dr[name] = p.dram(name, [4096, T_FULL], F32)
    dr["s_ucb"] = p.dram("s_ucb", [4096, T_FULL], BF16)
    dr["s_xdbl"] = p.dram("s_xdbl", [96, T_FULL], F32)

    own0 = T_HALF
    own1 = T_FULL

    ar = Arena(p)
    ar.load_consts(p, dr)
    tcm = p.dma("sync", ar.cf[:, 2:3], dr["cmask128"])
    tcn = p.dma("sync", ar.cf[:, 3:4], dr["cmn128"])
    p.barrier()
    if _go(): phase_copy_stream(p, ar, dr, 0, T_FULL, "x_in", "s_x")
    if _go(): phase_ln_stream(p, ar, dr, 0, T_FULL, src="s_x", dst="s_att")
    for l in range(6):
        if _go(): phase_att_layer(p, ar, dr, l, 0, T_FULL)
    if _go(): phase_add_ln(p, ar, dr, 0, T_FULL, x="s_x", slot="s_att", dst="s_x")
    if _go(): phase_ln_stream(p, ar, dr, 0, T_FULL, src="s_x", dst="s_f")
    if _go(): phase_ffn0(p, ar, dr, 0, T_FULL)
    if _go(): phase_add_ln(p, ar, dr, 0, T_FULL, x="s_x", slot="s_z", dst="s_x")
    if _go(): phase_ln_stream(p, ar, dr, 0, T_FULL, src="s_x", dst="s_f")
    dr2 = dict(dr)
    dr2["s_in"] = dr["s_f"]
    dr2["s_out"] = dr["s_a"]
    if _go(): phase_mlp_a(p, ar, dr2, 0, T_FULL)
    if _go(): phase_mlp_b(p, ar, dr2, 0, T_FULL)
    if _go(): phase_proj_fm(p, ar, dr, 0, T_FULL, "s_a", "s_uraw",
                            dr["ipT"][:, 0:4096],
                            dr["g2c"], dr["b2c"], None, None)
    if _go(): phase_proj_fm(p, ar, dr, 0, T_FULL, "s_a", "s_mz",
                            dr["ipT"][:, 4096:8192],
                            dr["g2c"], dr["b2c"], None, None)
    if _go(): phase_conv(p, ar, dr, 0, T_FULL)
    if _go(): phase_dt(p, ar, dr, 0, T_FULL)
    if _go(): phase_scan_setup(p, ar, dr)
    if _go(): phase_scan(p, ar, dr, 0, T_HALF, yout=False,
                          bscale_ap=ar.cf[:, 3:4])
    if _go(): phase_scan(p, ar, dr, T_HALF, T_FULL, yout=True)
    if _go(): phase_postscan(p, ar, dr, own0, own1)
    if _go(): phase_add_ln(p, ar, dr, own0, own1, x="s_x", slot="s_a3", dst="s_x")
    if _go(): phase_ln_stream(p, ar, dr, own0, own1, src="s_x", dst="s_f")
    if _go(): phase_ffn1(p, ar, dr, own0, own1)
    if _go(): phase_add_ln(p, ar, dr, own0, own1, x="s_x", slot="s_z", dst="out",
                           dst_off=own0)
    if dump is not None:
        shp = [dr[dump].shape[0], dr[dump].shape[1]]
        dr["dbg_out"] = p.dram("dbg_out", shp, dr[dump].dtype,
                               kind="ExternalOutput")
        prev = None
        for r0 in range(0, shp[0], 128):
            t = p.dma("sync", ar.fa[:, 0:shp[1]] if shp[1] <= 8192 else None,
                      dr[dump][r0:r0 + 128, :], deps=[prev] if prev else ())
            prev = p.dma("scalar", dr["dbg_out"][r0:r0 + 128, :],
                         ar.fa[:, 0:shp[1]], deps=[t])
    p.barrier()
    p.replay()
    return nc


_PERM0 = np.concatenate([np.arange(0, 768), np.arange(1536, 2304),
                         np.arange(768, 1536), np.arange(2304, 3072)])


def _percol(v):
    n = v.shape[0] // 128
    return np.ascontiguousarray(v.reshape(n, 128).T)


def _host_prep(inp):
    C = np.ascontiguousarray
    d = {}
    d["c_eye"] = np.eye(128, dtype=np.float32)
    d["c_ones"] = np.ones((1, 512), np.float32)
    d["w_qkvF"] = inp["L0_attn_w_qkv"].reshape(6 * D, 3072)
    d["w_outF"] = inp["L0_attn_w_out"].reshape(6 * D, D)
    d["b_outF"] = inp["L0_attn_b_out"]
    d["l0g"] = np.concatenate([_percol(inp["L0_attn_gamma"][l])
                               for l in range(6)], axis=1)
    d["l0b"] = np.concatenate([_percol(inp["L0_attn_beta"][l])
                               for l in range(6)], axis=1)
    d["gate_wT"] = C(inp["L0_ffn_gate_w"].T)
    d["W1Tf"] = C(inp["L0_ffn_W1"].transpose(0, 2, 1)).reshape(8 * D, 1536)
    d["W2Tf"] = C(inp["L0_ffn_W2"].transpose(0, 2, 1)).reshape(8 * 768, D)
    d["Ws1P"] = C(inp["L0_ffn_Ws1"].T[:, _PERM0])
    d["Ws2P"] = C(inp["L0_ffn_Ws2"].T)
    d["g1c"] = _percol(inp["L1_attn_g1"])
    d["b1c"] = _percol(inp["L1_attn_b1"])
    d["w1T"] = C(inp["L1_attn_w1"].T)
    d["bias1"] = inp["L1_attn_bias1"]
    d["w2T"] = C(inp["L1_attn_w2"].T)
    d["bias2"] = inp["L1_attn_bias2"]
    d["ipT"] = C(inp["L1_attn_in_proj_w"].T)
    d["g2c"] = _percol(inp["L1_attn_g2"])
    d["b2c"] = _percol(inp["L1_attn_b2"])
    d["conv_wc"] = C(inp["L1_attn_conv_w"].reshape(32, 128, 4)
                     .transpose(1, 0, 2).reshape(128, 128))
    d["conv_bc"] = _percol(inp["L1_attn_conv_b"])
    d["x_projT"] = C(inp["L1_attn_x_proj_w"].T)
    d["dt_projT"] = C(inp["L1_attn_dt_proj_w"].T)
    d["dt_proj_b"] = inp["L1_attn_dt_proj_b"]
    d["Alogc"] = C(inp["L1_attn_A_log"].reshape(32, 128, 16)
                   .transpose(1, 0, 2).reshape(128, 512))
    sel = np.zeros((32, 4096), np.float32)
    for g in range(32):
        sel[g, g * 128:(g + 1) * 128] = 1.0
    d["sel16"] = sel
    d["Dc"] = _percol(inp["L1_attn_D"])
    d["out_projT"] = C(inp["L1_attn_out_proj_w"].T)
    d["gWT"] = inp["L1_ffn_gW"]
    d["gb"] = inp["L1_ffn_gb"]
    d["rW1f"] = inp["L1_ffn_rW1"].reshape(15 * D, 2048)
    d["rb1f"] = inp["L1_ffn_rb1"].reshape(-1)
    d["rW2f"] = inp["L1_ffn_rW2"].reshape(15 * 2048, D)
    d["rb2f"] = inp["L1_ffn_rb2"].reshape(-1)
    d["sW1m"] = inp["L1_ffn_sW1"][0]
    d["sb1"] = inp["L1_ffn_sb1"][0]
    d["sW2m"] = inp["L1_ffn_sW2"][0]
    d["sb2"] = inp["L1_ffn_sb2"][0]
    return {k: np.ascontiguousarray(v, dtype=np.float32) for k, v in d.items()}


def kernel(**inputs):
    w = _host_prep(inputs)
    x = np.ascontiguousarray(inputs["L0_attn_x"], dtype=np.float32)
    nc = build_core_program()
    in_maps = []
    for c in range(8):
        b, h = c // 2, c % 2
        m = dict(w)
        if h == 0:
            m["x_in"] = np.ascontiguousarray(
                np.concatenate([x[b, T_HALF:], x[b, :T_HALF]], axis=0))
            m["cmask128"] = np.zeros((128, 1), np.float32)
            m["cmn128"] = np.zeros((128, 1), np.float32)
        else:
            m["x_in"] = x[b]
            m["cmask128"] = np.ones((128, 1), np.float32)
            m["cmn128"] = np.full((128, 1), -1.0, np.float32)
        in_maps.append(m)
    res = run_bass_kernel_spmd(nc, in_maps, core_ids=list(range(8)))
    out = np.empty((4, T_FULL, D), np.float32)
    for c in range(8):
        b, h = c // 2, c % 2
        out[b, h * T_HALF:(h + 1) * T_HALF] = res.results[c]["out"]
    return out


# version 13
# speedup vs baseline: 1.0099x; 1.0099x over previous
import contextlib
import numpy as np
import concourse.bass as bass
import concourse.mybir as mybir
from concourse.bass_utils import run_bass_kernel_spmd

F32 = mybir.dt.float32
BF16 = mybir.dt.bfloat16
AF = mybir.ActivationFunctionType
ALU = mybir.AluOpType
AX = mybir.AxisListType

CENGS = ("tensor", "vector", "scalar", "gpsimd", "sync")
QUEUES = ("sync", "scalar", "gpsimd")
NSLOT = 3

D = 1024
T_FULL = 8192
T_HALF = 4096
LN_EPS = 1e-5



class Prog:
    def __init__(self, nc):
        self.nc = nc
        self._es = contextlib.ExitStack()
        self.q = {e: [] for e in CENGS}
        self.sems = {}
        self.cnt = {}
        self.marks = {}
        self.drams = {}
        for e in CENGS:
            self._mksem("c_" + e)
        self.qslot = {}
        for q in QUEUES:
            self.qslot[q] = 0
            for i in range(NSLOT):
                self._mksem("q_%s%d" % (q, i))
        self._mksem("bar")

    def _mksem(self, name):
        self.sems[name] = self._es.enter_context(self.nc.semaphore(name))
        self.cnt[name] = 0

    def sb(self, name, shape, dtype):
        return self._es.enter_context(self.nc.sbuf_tensor(name, shape, dtype))

    def psum(self, name, shape, dtype=F32):
        return self._es.enter_context(self.nc.psum_tensor(name, shape, dtype))

    def dram(self, name, shape, dtype=F32, kind="Internal"):
        t = self.nc.dram_tensor(name, shape, dtype, kind=kind)
        self.drams[name] = t
        return t.ap()

    def _collect_waits(self, consumer, deps, selfwait):
        own = "c_" + consumer
        waits = []
        for d in deps:
            if d is None:
                continue
            sem, tgt = d
            if sem == own and not selfwait:
                continue
            key = (consumer, sem)
            if self.marks.get(key, -1) >= tgt:
                continue
            self.marks[key] = tgt
            waits.append((self.sems[sem], tgt))
        return waits

    def op(self, eng, fn, deps=(), selfwait=True):
        sem = "c_" + eng
        waits = self._collect_waits(eng, deps, selfwait)
        h = self.sems[sem]

        def run(e, fn=fn, waits=waits, h=h):
            for wh, wt in waits:
                e.wait_ge(wh, wt)
            inst = fn(e)
            inst.then_inc(h, 1)
        self.q[eng].append(run)
        self.cnt[sem] += 1
        return (sem, self.cnt[sem])

    def _dma_common(self, queue, deps):
        slot = self.qslot[queue]
        self.qslot[queue] = (slot + 1) % NSLOT
        sem = "q_%s%d" % (queue, slot)
        alldeps = list(deps)
        if self.cnt[sem] > 0:
            alldeps.append((sem, self.cnt[sem]))
        waits = self._collect_waits(queue, alldeps, selfwait=True)
        return sem, waits

    def dma(self, queue, out, in_, deps=(), accum=False, allow_slow=False):
        sem, waits = self._dma_common(queue, deps)
        h = self.sems[sem]

        def run(e, out=out, in_=in_, waits=waits, h=h):
            for wh, wt in waits:
                e.wait_ge(wh, wt)
            kw = {}
            if accum:
                kw["accum_op"] = ALU.add
            if allow_slow:
                kw["allow_slow_non_contiguous"] = True
            e.dma_start(out=out, in_=in_, **kw).then_inc(h, 16)
        self.q[queue].append(run)
        self.cnt[sem] += 16
        return (sem, self.cnt[sem])

    def dmaT(self, queue, out, in_, deps=()):
        sem, waits = self._dma_common(queue, deps)
        h = self.sems[sem]

        def run(e, out=out, in_=in_, waits=waits, h=h):
            for wh, wt in waits:
                e.wait_ge(wh, wt)
            e.dma_start_transpose(out, in_).then_inc(h, 16)
        self.q[queue].append(run)
        self.cnt[sem] += 16
        return (sem, self.cnt[sem])

    def barrier(self):
        deps = [("c_" + e, self.cnt["c_" + e]) for e in CENGS]
        deps += [("q_%s%d" % (q, i), self.cnt["q_%s%d" % (q, i)])
                 for q in QUEUES for i in range(NSLOT)]
        h = self.sems["bar"]
        self.cnt["bar"] += len(CENGS)
        tgt = self.cnt["bar"]
        for e in CENGS:
            waits = self._collect_waits(e, deps, selfwait=False)

            def run(eng, waits=waits, h=h, tgt=tgt):
                for wh, wt in waits:
                    eng.wait_ge(wh, wt)
                eng.nop().then_inc(h, 1)
                eng.wait_ge(h, tgt)
            self.q[e].append(run)
            for (sem, t) in deps:
                self.marks[(e, sem)] = max(self.marks.get((e, sem), -1), t)
        return ("bar", tgt)

    def replay(self):
        with self.nc.Block() as block:
            @block.sync
            def _(e):
                for fn in self.q["sync"]:
                    fn(e)

            @block.scalar
            def _(e):
                for fn in self.q["scalar"]:
                    fn(e)

            @block.vector
            def _(e):
                for fn in self.q["vector"]:
                    fn(e)

            @block.tensor
            def _(e):
                for fn in self.q["tensor"]:
                    fn(e)

            @block.gpsimd
            def _(e):
                for fn in self.q["gpsimd"]:
                    fn(e)

    def close(self):
        self._es.close()



class Arena:
    def __init__(self, p):
        self.wa = p.sb("wa", [128, 32768], BF16)
        self.fa = p.sb("fa", [128, 8192], F32)
        self.fb = p.sb("fb", [128, 4096], F32)
        self.ba = p.sb("ba", [128, 16384], BF16)
        self.bb = p.sb("bb", [128, 8192], BF16)
        self.fs = p.sb("fs", [128, 3072], F32)
        self.ms = p.sb("ms", [128, 1024], F32)
        self.eye = p.sb("eye", [128, 128], BF16)
        self.eyef = p.sb("eyef", [128, 128], F32)
        self.ones = p.sb("ones", [1, 512], BF16)
        self.br = p.sb("br", [1, 4096], BF16)
        self.brf = p.sb("brf", [1, 4096], F32)
        self.cf = p.sb("cf", [128, 8], F32)
        self.pa = p.psum("pa", [128, 3072], F32)
        self.pb = p.psum("pb", [128, 2048], BF16)

    def load_consts(self, p, dr):
        p.dma("gpsimd", self.eye[:, :], dr["c_eye"])
        p.dma("sync", self.eyef[:, :], dr["c_eye"])
        p.dma("gpsimd", self.ones[0:1, :], dr["c_ones"])
        p.op("vector", lambda e: e.memset(self.cf[:, 0:1], LN_EPS))
        p.barrier()



def mm_accum(p, out_psum, pairs, deps=()):
    def f(e):
        last = None
        n = len(pairs)
        for i, (l, r) in enumerate(pairs):
            last = e.matmul(out_psum, l, r, start=(i == 0), stop=(i == n - 1))
        return last
    return p.op("tensor", f, deps=deps, selfwait=False)


def tr128(p, ar, dst, src, deps, slot, prev_evac, evac="vector"):
    pb = ar.pb
    tdeps = list(deps)
    if prev_evac is not None:
        tdeps.append(prev_evac)
    t1 = p.op("tensor",
              lambda e, dst=dst, src=src, slot=slot:
              e.transpose(pb[:, slot * 1024:slot * 1024 + 128], src, ar.eye[:, :]),
              deps=tdeps, selfwait=False)
    if evac == "scalar":
        t2 = p.op(evac,
                  lambda e, dst=dst, slot=slot:
                  e.copy(dst, pb[:, slot * 1024:slot * 1024 + 128]),
                  deps=[t1], selfwait=False)
    else:
        t2 = p.op(evac,
                  lambda e, dst=dst, slot=slot:
                  e.tensor_copy(dst, pb[:, slot * 1024:slot * 1024 + 128]),
                  deps=[t1], selfwait=False)
    return t2


def ln_chunk(p, src, dst, stats, dep, nsub=4, width=D, eps_ap=None,
             out_deps=()):
    tsum = [p.op("vector",
                 lambda e, i=i: e.reduce_sum(
                     stats[:, 6 * i:6 * i + 1],
                     src[:, i * width:(i + 1) * width], axis=AX.X),
                 deps=[dep] if i == 0 else (), selfwait=False)
            for i in range(nsub)]
    tsq = [p.op("scalar",
                lambda e, i=i: e.activation(
                    dst[:, i * width:(i + 1) * width],
                    src[:, i * width:(i + 1) * width],
                    AF.Square, accum_out=stats[:, 6 * i + 1:6 * i + 2]),
                deps=([dep] + list(out_deps)) if i == 0 else (),
                selfwait=False)
           for i in range(nsub)]
    tmu = [p.op("gpsimd",
                lambda e, i=i: e.tensor_scalar(
                    stats[:, 6 * i + 2:6 * i + 3], stats[:, 6 * i:6 * i + 1],
                    -1.0 / width, None, ALU.mult),
                deps=[tsum[i]], selfwait=False) for i in range(nsub)]
    tv = [p.op("vector",
               lambda e, i=i: e.scalar_tensor_tensor(
                   stats[:, 6 * i + 3:6 * i + 4], stats[:, 6 * i:6 * i + 1],
                   stats[:, 6 * i + 2:6 * i + 3], stats[:, 6 * i + 1:6 * i + 2],
                   ALU.mult, ALU.add),
               deps=[tsum[i], tsq[i], tmu[i]], selfwait=True)
          for i in range(nsub)]
    tr = []
    for i in range(nsub):
        tl = p.op("scalar",
                  lambda e, i=i: e.activation(
                      stats[:, 6 * i + 4:6 * i + 5],
                      stats[:, 6 * i + 3:6 * i + 4],
                      AF.Ln, bias=eps_ap, scale=1.0 / width),
                  deps=[tv[i]], selfwait=True)
        tr.append(p.op("scalar",
                       lambda e, i=i: e.activation(
                           stats[:, 6 * i + 5:6 * i + 6],
                           stats[:, 6 * i + 4:6 * i + 5],
                           AF.Exp, scale=-0.5),
                       deps=[tl], selfwait=True))
    outs = [p.op("vector",
                 lambda e, i=i: e.tensor_scalar(
                     dst[:, i * width:(i + 1) * width],
                     src[:, i * width:(i + 1) * width],
                     stats[:, 6 * i + 2:6 * i + 3],
                     stats[:, 6 * i + 5:6 * i + 6],
                     ALU.add, ALU.mult),
                 deps=[tr[i]] + (list(out_deps) if i == 0 else []),
                 selfwait=(i == 0)) for i in range(nsub)]
    return outs[-1]


def load_wT_bf16(p, dst, w_hbm, kdim, ndim, deps=(), col0=0):
    toks = []
    nk = kdim // 128
    for j in range(nk):
        toks.append(p.dma("gpsimd", dst[:, col0 + j * ndim: col0 + (j + 1) * ndim],
                          w_hbm[j * 128:(j + 1) * 128, :], deps=deps))
    return toks


def row2d(ap):
    return ap.rearrange("(a n) -> a n", a=1)


def fold_ln_into_w(p, ar, wsb, nk, ndim, g_tile, b_tile, bias_hbm, brow_bf,
                   brow_cols, deps, col0=0):
    tb = None
    if bias_hbm is not None:
        tb = p.dma("sync", ar.brf[0:1, 0:ndim], row2d(bias_hbm), deps=())
    nslice = ndim // 512
    row_toks = []
    prev_evac = None
    for s in range(nslice):
        pairs = [(b_tile[:, j:j + 1],
                  wsb[:, col0 + j * ndim + s * 512: col0 + j * ndim + s * 512 + 512])
                 for j in range(nk)]
        mdeps = list(deps) + ([prev_evac] if prev_evac else [])
        tmm = mm_accum(p, ar.pa[0:1, 0:512], pairs, deps=mdeps)
        if bias_hbm is not None:
            tev = p.op("vector",
                       lambda e, s=s: e.tensor_add(
                           ar.brf[0:1, s * 512:(s + 1) * 512],
                           ar.brf[0:1, s * 512:(s + 1) * 512],
                           ar.pa[0:1, 0:512]),
                       deps=[tmm, tb], selfwait=False)
            tcast = p.op("vector",
                         lambda e, s=s: e.tensor_copy(
                             brow_bf[0:1, brow_cols + s * 512:
                                     brow_cols + (s + 1) * 512],
                             ar.brf[0:1, s * 512:(s + 1) * 512]),
                         deps=[tev], selfwait=True)
        else:
            tcast = p.op("vector",
                         lambda e, s=s: e.tensor_copy(
                             brow_bf[0:1, brow_cols + s * 512:
                                     brow_cols + (s + 1) * 512],
                             ar.pa[0:1, 0:512]),
                         deps=[tmm], selfwait=False)
        prev_evac = tcast
        row_toks.append(tcast)
    w_toks = []
    for j in range(nk):
        w_toks.append(p.op("vector",
                           lambda e, j=j: e.tensor_scalar(
                               wsb[:, col0 + j * ndim: col0 + (j + 1) * ndim],
                               wsb[:, col0 + j * ndim: col0 + (j + 1) * ndim],
                               g_tile[:, j:j + 1], None, ALU.mult),
                           deps=row_toks, selfwait=False))
    return w_toks, row_toks[-1]


def stream_chunk_ap(s_ap, t0, ntok=512, width=D):
    return s_ap[t0:t0 + ntok, :].rearrange("(n p) d -> p n d", p=128)


def sb3(tile_slice, width):
    return tile_slice.rearrange("p (n d) -> p n d", d=width)


def fm_block_ap(s_ap, r0, nrow_chunks, c0, ncol):
    return s_ap[r0:r0 + 128 * nrow_chunks, c0:c0 + ncol].rearrange(
        "(c p) t -> p c t", p=128)



def phase_proj_fm(p, ar, dr, t0, t1, src_key, dst_key, w_ap, g_ap, b_ap,
                  bias_ap, act, stage=99):
    wtoks = load_wT_bf16(p, ar.wa, w_ap, D, 4096)
    tg = p.dma("sync", ar.fs[:, 0:8], g_ap)
    tb = p.dma("gpsimd", ar.bb[:, 0:8], b_ap)
    if stage == 0:
        p.barrier()
        return
    w_toks, brow_tok = fold_ln_into_w(
        p, ar, ar.wa, 8, 4096, ar.fs[:, 0:8], ar.bb[:, 0:8],
        bias_ap, ar.br, 0, deps=wtoks + [tg, tb])
    p.barrier()
    if stage == 1:
        return

    prev_ln = [None, None]
    prev_tr = None
    prev_mm = None
    hdmas_tail = [None] * 4
    evac_of_slot = [None, None, None, None]
    for ck in range((t1 - t0) // 512):
        tt = t0 + ck * 512
        par = ck % 2
        xt = ar.fa[:, 0:4096] if par == 0 else ar.fb[:, 0:4096]
        tX = p.dma("sync", sb3(xt, D), stream_chunk_ap(dr[src_key], tt),
                   deps=[prev_ln[par]])
        tLN = ln_chunk(p, xt, ar.ba[:, 0:4096],
                       ar.ms[:, 24 * par:24 * par + 24], tX,
                       eps_ap=ar.cf[:, 0:1], out_deps=[prev_tr])
        prev_ln[par] = tLN
        ev = []
        prev = [None, None]
        for i in range(4):
            for j in range(8):
                k = i * 8 + j
                tdeps = [tLN]
                if k < 2 and prev_mm is not None:
                    tdeps.append(prev_mm)
                tev = tr128(p, ar,
                            ar.bb[:, j * 512 + i * 128: j * 512 + i * 128 + 128],
                            ar.ba[:, i * 1024 + j * 128: i * 1024 + j * 128 + 128],
                            tdeps, k % 2, prev[k % 2],
                            evac="scalar" if k % 2 else "vector")
                prev[k % 2] = tev
                ev.append(tev)
        prev_tr = ev[-1]
        if stage == 2:
            p.barrier()
            continue
        hdmas = []
        for eg in range(4):
            htoks = []
            if act == "silu":
                hbase = 4096 + (eg % 2) * 4096
            else:
                hbase = 4096
            for el in range(8):
                ec = eg * 8 + el
                pairs = [(ar.wa[:, j * 4096 + ec * 128: j * 4096 + ec * 128 + 128],
                          ar.bb[:, j * 512:(j + 1) * 512]) for j in range(8)]
                pairs.append((ar.br[0:1, ec * 128:ec * 128 + 128],
                              ar.ones[0:1, 0:512]))
                mdeps = ev + w_toks + [brow_tok]
                if evac_of_slot[ec % 4] is not None:
                    mdeps.append(evac_of_slot[ec % 4])
                tmm = mm_accum(p, ar.pa[:, (ec % 4) * 512:(ec % 4) * 512 + 512],
                               pairs, deps=mdeps)
                sdeps = [tmm]
                if act == "silu":
                    sdeps.append(hdmas_tail[(eg + 2) % 4])
                    tsil = p.op("scalar",
                                lambda e, ec=ec, el=el, hbase=hbase: e.activation(
                                    ar.ba[:, hbase + el * 512: hbase + el * 512 + 512],
                                    ar.pa[:, (ec % 4) * 512:(ec % 4) * 512 + 512],
                                    AF.Silu),
                                deps=sdeps, selfwait=False)
                else:
                    sdeps.append(hdmas_tail[(eg + 3) % 4])
                    tsil = p.op("scalar",
                                lambda e, ec=ec, el=el, hbase=hbase: e.copy(
                                    ar.fa[:, hbase + el * 512: hbase + el * 512 + 512],
                                    ar.pa[:, (ec % 4) * 512:(ec % 4) * 512 + 512]),
                                deps=sdeps, selfwait=False)
                evac_of_slot[ec % 4] = tsil
                htoks.append(tsil)
                prev_mm = tmm
            if act == "silu":
                hd = p.dma("sync",
                           fm_block_ap(dr[dst_key], eg * 1024, 8, tt, 512),
                           sb3(ar.ba[:, hbase:hbase + 4096], 512),
                           deps=htoks)
            else:
                hd = p.dma("sync",
                           fm_block_ap(dr[dst_key], eg * 1024, 8, tt, 512),
                           sb3(ar.fa[:, 4096:8192], 512), deps=htoks)
            hdmas.append(hd)
            hdmas_tail[eg] = hd
    p.barrier()
    return


def phase_mlp_a(p, ar, dr, t0, t1, stage=99):
    phase_proj_fm(p, ar, dr, t0, t1, "s_in", "s_h", dr["w1T"], dr["g1c"],
                  dr["b1c"], dr["bias1"], "silu", stage=stage)



def phase_ln_stream(p, ar, dr, t0, t1, src="s_in", dst="s_out"):
    prev_ln = [None, None]
    prev_st = None
    for ck in range((t1 - t0) // 512):
        tt = t0 + ck * 512
        par = ck % 2
        xt = ar.fa[:, par * 4096:par * 4096 + 4096]
        tX = p.dma("sync", sb3(xt, D), stream_chunk_ap(dr[src], tt),
                   deps=[prev_ln[par]])
        tLN = ln_chunk(p, xt, ar.fb[:, 0:4096],
                       ar.ms[:, 24 * par:24 * par + 24], tX,
                       eps_ap=ar.cf[:, 0:1], out_deps=[prev_st])
        prev_ln[par] = tLN
        prev_st = p.dma("scalar", stream_chunk_ap(dr[dst], tt),
                        sb3(ar.fb[:, 0:4096], D), deps=[tLN])
    p.barrier()



def phase_add_ln(p, ar, dr, t0, t1, x="s_x", slot="s_slot", dst="s_out",
                 dst_off=0):
    prev_ln = None
    prev_add = None
    prev_st = None
    for ck in range((t1 - t0) // 512):
        tt = t0 + ck * 512
        tS = p.dma("sync", sb3(ar.fa[:, 0:4096], D),
                   stream_chunk_ap(dr[slot], tt), deps=[prev_ln])
        tX = p.dma("sync", sb3(ar.fb[:, 0:4096], D),
                   stream_chunk_ap(dr[x], tt), deps=[prev_st])
        tLN = ln_chunk(p, ar.fa[:, 0:4096], ar.fa[:, 4096:8192],
                       ar.ms[:, 0:24], tS, eps_ap=ar.cf[:, 0:1],
                       out_deps=[prev_add])
        prev_ln = tLN
        adds = [p.op("vector",
                     lambda e, i=i: e.tensor_add(
                         ar.fb[:, i * 1024:(i + 1) * 1024],
                         ar.fb[:, i * 1024:(i + 1) * 1024],
                         ar.fa[:, 4096 + i * 1024: 4096 + (i + 1) * 1024]),
                     deps=[tLN, tX], selfwait=True) for i in range(4)]
        prev_add = adds[-1]
        prev_st = p.dma("scalar", stream_chunk_ap(dr[dst], tt - dst_off),
                        sb3(ar.fb[:, 0:4096], D), deps=adds)
    p.barrier()



def phase_mlp_b(p, ar, dr, t0, t1):
    wtoks = load_wT_bf16(p, ar.wa, dr["w2T"], 4096, D)
    tb2 = p.dma("gpsimd", ar.br[0:1, 0:D], row2d(dr["bias2"]))
    p.barrier()

    prev_evac = [None, None, None, None]
    prev_mm_last = None
    prev_adds = None
    prev_st = None
    for ck in range((t1 - t0) // 512):
        tt = t0 + ck * 512
        tH = p.dma("sync", sb3(ar.ba[:, 0:32 * 512], 512),
                   fm_block_ap(dr["s_h"], 0, 32, tt, 512),
                   deps=[prev_mm_last])
        tX = p.dma("sync", sb3(ar.fa[:, 0:4096], D),
                   stream_chunk_ap(dr["s_in"], tt), deps=[prev_adds])
        otoks = []
        for ts_ in range(4):
            for nh in range(2):
                slot = (ts_ % 2) * 2 + nh
                pairs = [(ar.ba[:, ec * 512 + ts_ * 128: ec * 512 + ts_ * 128 + 128],
                          ar.wa[:, ec * 1024 + nh * 512: ec * 1024 + nh * 512 + 512])
                         for ec in range(32)]
                pairs.append((ar.ones[0:1, 0:128],
                              ar.br[0:1, nh * 512:(nh + 1) * 512]))
                mdeps = wtoks + [tH, tb2, tX]
                if prev_evac[slot] is not None:
                    mdeps.append(prev_evac[slot])
                tmm = mm_accum(p, ar.pa[:, slot * 512: slot * 512 + 512],
                               pairs, deps=mdeps)
                prev_mm_last = tmm
                tev = p.op("vector",
                           lambda e, ts_=ts_, nh=nh, slot=slot: e.tensor_add(
                               ar.fb[:, ts_ * 1024 + nh * 512:
                                     ts_ * 1024 + nh * 512 + 512],
                               ar.pa[:, slot * 512: slot * 512 + 512],
                               ar.fa[:, ts_ * 1024 + nh * 512:
                                     ts_ * 1024 + nh * 512 + 512]),
                           deps=[tmm] + ([prev_st] if (ts_ == 0 and nh == 0)
                                         else []),
                           selfwait=False)
                prev_evac[slot] = tev
                otoks.append(tev)
        prev_adds = otoks[-1]
        prev_st = p.dma("scalar", stream_chunk_ap(dr["s_out"], tt),
                        sb3(ar.fb[:, 0:4096], D), deps=otoks)
    p.barrier()
    return



def phase_att_setup(p, ar, dr, l):
    wq = load_wT_bf16(p, ar.wa, dr["w_qkvF"][l * 1024:(l + 1) * 1024, :],
                      D, 3072, col0=0)
    wo = load_wT_bf16(p, ar.wa, dr["w_outF"][l * 1024:(l + 1) * 1024, :],
                      D, 1024, col0=24576)
    tg = p.dma("sync", ar.fs[:, 0:8], dr["l0g"][:, l * 8:(l + 1) * 8])
    tb = p.dma("gpsimd", ar.bb[:, 0:8], dr["l0b"][:, l * 8:(l + 1) * 8])
    w_toks, brow_tok = fold_ln_into_w(
        p, ar, ar.wa, 8, 3072, ar.fs[:, 0:8], ar.bb[:, 0:8],
        None, ar.br, 0, deps=wq + [tg, tb])
    tbo = p.dma("gpsimd", ar.br[0:1, 3072:4096], dr["b_outF"][l:l + 1, :])
    m1 = p.op("vector", lambda e: e.memset(ar.fs[:, 8:16], -1e30))
    m2 = p.op("vector", lambda e: e.memset(ar.fs[:, 16:24], 0.0))
    m3 = p.op("vector", lambda e: e.memset(ar.fa[:, :], 0.0))
    return w_toks + wo + [brow_tok, tbo, m1, m2, m3]


def phase_att_pass1(p, ar, dr, t0, t1, setup_deps=()):
    first = True
    pv = {}
    pv["setup"] = list(setup_deps)
    gprev = [None, None]
    for ck in range((t1 - t0) // 512):
        tt = t0 + ck * 512
        tX = p.dma("sync", sb3(ar.fb[:, 0:4096], D),
                   stream_chunk_ap(dr["s_att"], tt),
                   deps=[pv.get("ln")])
        tLN = ln_chunk(p, ar.fb[:, 0:4096], ar.ba[:, 0:4096],
                       ar.ms[:, 0:24], tX, eps_ap=ar.cf[:, 0:1],
                       out_deps=[pv.get("trexp")])
        pv["ln"] = tLN
        ev = []
        prev = [None, None]
        for i in range(4):
            for j in range(8):
                k = i * 8 + j
                tev = tr128(p, ar,
                            ar.bb[:, j * 512 + i * 128: j * 512 + i * 128 + 128],
                            ar.ba[:, i * 1024 + j * 128: i * 1024 + j * 128 + 128],
                            [tLN], k % 2, prev[k % 2],
                            evac="scalar" if k % 2 else "vector")
                prev[k % 2] = tev
                ev.append(tev)
        exp_toks = [None] * 8
        for half in range(2):
            mm_toks = []
            cmax_toks = []
            for jcl in range(4):
                jc = half * 4 + jcl
                pairs = [(ar.wa[:, jk * 3072 + jc * 128: jk * 3072 + jc * 128 + 128],
                          ar.bb[:, jk * 512:(jk + 1) * 512]) for jk in range(8)]
                pairs.append((ar.br[0:1, jc * 128:jc * 128 + 128],
                              ar.ones[0:1, 0:512]))
                mdeps = list(ev) + pv["setup"]
                if exp_toks[jcl] is not None:
                    mdeps.append(exp_toks[jcl])
                elif pv.get("vev") is not None:
                    mdeps.append(pv["vev"][jcl])
                tmm = mm_accum(p, ar.pa[:, jcl * 512:(jcl + 1) * 512], pairs,
                               deps=mdeps)
                mm_toks.append(tmm)
                cdeps = [tmm]
                if half == 0 and jcl == 0 and pv.get("exp_last") is not None:
                    cdeps.append(pv["exp_last"])
                cmax_toks.append(p.op(
                    "vector",
                    lambda e, jc=jc, jcl=jcl: e.reduce_max(
                        ar.ms[:, 40 + jc:41 + jc],
                        ar.pa[:, jcl * 512:(jcl + 1) * 512], axis=AX.X),
                    deps=cdeps, selfwait=False))
            c0 = 8 + half * 4
            tmn = p.op("vector",
                       lambda e, half=half, c0=c0: e.tensor_max(
                           ar.ms[:, 48 + half * 4:52 + half * 4],
                           ar.fs[:, c0:c0 + 4],
                           ar.ms[:, 40 + half * 4:44 + half * 4]),
                       deps=cmax_toks, selfwait=True)
            tdd = p.op("vector",
                       lambda e, half=half, c0=c0: e.tensor_sub(
                           ar.ms[:, 64 + half * 4:68 + half * 4],
                           ar.fs[:, c0:c0 + 4],
                           ar.ms[:, 48 + half * 4:52 + half * 4]),
                       deps=[tmn], selfwait=True)
            tcm = p.op("vector",
                       lambda e, half=half, c0=c0: e.tensor_copy(
                           ar.fs[:, c0:c0 + 4],
                           ar.ms[:, 48 + half * 4:52 + half * 4]),
                       deps=[tdd], selfwait=True)
            tng = p.op("vector",
                       lambda e, half=half: e.tensor_scalar(
                           ar.ms[:, 72 + half * 4:76 + half * 4],
                           ar.ms[:, 48 + half * 4:52 + half * 4],
                           -1.0, None, ALU.mult),
                       deps=[tmn], selfwait=True)
            tsj = p.op("scalar",
                       lambda e, half=half: e.activation(
                           ar.ms[:, 56 + half * 4:60 + half * 4],
                           ar.ms[:, 64 + half * 4:68 + half * 4], AF.Exp),
                       deps=[tdd], selfwait=False)
            for jcl in range(4):
                jc = half * 4 + jcl
                texp = p.op("scalar",
                            lambda e, jc=jc, jcl=jcl: e.activation(
                                ar.ba[:, jc * 512:(jc + 1) * 512],
                                ar.pa[:, jcl * 512:(jcl + 1) * 512],
                                AF.Exp, bias=ar.ms[:, 72 + jc:73 + jc],
                                accum_out=ar.ms[:, 80 + jc:81 + jc]),
                            deps=[tng, mm_toks[jcl]], selfwait=True)
                exp_toks[jc] = texp
            if half == 0:
                sj_tok0 = tsj
            else:
                sj_tok1 = tsj
        pv["exp_last"] = exp_toks[7]
        tz1 = p.op("vector",
                   lambda e: e.tensor_mul(ar.fs[:, 16:24], ar.fs[:, 16:24],
                                          ar.ms[:, 56:64]),
                   deps=[sj_tok0, sj_tok1], selfwait=True)
        tz2 = p.op("vector",
                   lambda e: e.tensor_add(ar.fs[:, 16:24], ar.fs[:, 16:24],
                                          ar.ms[:, 80:88]),
                   deps=[tz1] + exp_toks, selfwait=True)
        ev2 = []
        prev2 = [None, None]
        for i in range(4):
            for jc in range(8):
                k = i * 8 + jc
                tdeps = [exp_toks[jc]]
                if k < 2 and pv.get("gmm") is not None:
                    tdeps.append(pv["gmm"])
                tev = tr128(p, ar,
                            ar.ba[:, 8192 + i * 1024 + jc * 128:
                                  8192 + i * 1024 + jc * 128 + 128],
                            ar.ba[:, jc * 512 + i * 128: jc * 512 + i * 128 + 128],
                            tdeps, k % 2, prev2[k % 2],
                            evac="scalar" if k % 2 else "vector")
                prev2[k % 2] = tev
                ev2.append(tev)
        pv["trexp"] = ev2[-1]
        kexp_prev = None
        knorm_toks = []
        kexp_toks = []
        for i in range(4):
            kmm = []
            for h in range(2):
                pairs = [(ar.bb[:, jk * 512 + i * 128: jk * 512 + i * 128 + 128],
                          ar.wa[:, jk * 3072 + 1024 + h * 512:
                                jk * 3072 + 1024 + h * 512 + 512])
                         for jk in range(8)]
                pairs.append((ar.ones[0:1, 0:128],
                              ar.br[0:1, 1024 + h * 512:1024 + h * 512 + 512]))
                mdeps = list(ev)
                if kexp_prev is not None:
                    mdeps.append(kexp_prev)
                elif pv.get("gstt") is not None:
                    mdeps.append(pv["gstt"])
                kmm.append(mm_accum(
                    p, ar.pa[:, 2048 + h * 512:2048 + h * 512 + 512],
                    pairs, deps=mdeps))
            tkm = p.op("vector",
                       lambda e, i=i: e.reduce_max(
                           ar.ms[:, 88 + i:89 + i], ar.pa[:, 2048:3072],
                           axis=AX.X, negate=True),
                       deps=kmm, selfwait=False)
            kedeps = [tkm]
            if i == 0 and pv.get("gmm") is not None:
                kedeps.append(pv["gmm"])
            tke = p.op("scalar",
                       lambda e, i=i: e.activation(
                           ar.ba[:, 4096 + i * 1024:4096 + (i + 1) * 1024],
                           ar.pa[:, 2048:3072], AF.Exp,
                           bias=ar.ms[:, 88 + i:89 + i],
                           accum_out=ar.ms[:, 92 + i:93 + i]),
                       deps=kedeps, selfwait=False)
            kexp_prev = tke
            kexp_toks.append(tke)
        trc = p.op("vector",
                   lambda e: e.reciprocal(ar.ms[:, 96:100], ar.ms[:, 92:96]),
                   deps=kexp_toks, selfwait=True)
        for i in range(4):
            knorm_toks.append(p.op(
                "vector",
                lambda e, i=i: e.tensor_scalar(
                    ar.ba[:, 4096 + i * 1024:4096 + (i + 1) * 1024],
                    ar.ba[:, 4096 + i * 1024:4096 + (i + 1) * 1024],
                    ar.ms[:, 96 + i:97 + i], None, ALU.mult),
                deps=[trc], selfwait=True))
        vev = []
        for jc in range(8):
            jcl = jc % 4
            pairs = [(ar.wa[:, jk * 3072 + 2048 + jc * 128:
                           jk * 3072 + 2048 + jc * 128 + 128],
                      ar.bb[:, jk * 512:(jk + 1) * 512]) for jk in range(8)]
            pairs.append((ar.br[0:1, 2048 + jc * 128:2048 + jc * 128 + 128],
                          ar.ones[0:1, 0:512]))
            mdeps = [exp_toks[jcl], exp_toks[4 + jcl]]
            if jc >= 4:
                mdeps.append(vev[jc - 4])
            tmm = mm_accum(p, ar.pa[:, jcl * 512:(jcl + 1) * 512], pairs,
                           deps=mdeps)
            vdeps = [tmm]
            if jc == 0 and pv.get("vdma") is not None:
                vdeps.append(pv["vdma"])
            vev.append(p.op("scalar",
                            lambda e, jc=jc, jcl=jcl: e.copy(
                                ar.ba[:, 12288 + jc * 512:12288 + (jc + 1) * 512],
                                ar.pa[:, jcl * 512:(jcl + 1) * 512]),
                            deps=vdeps, selfwait=False))
        pv["vev"] = [vev[4], vev[5], vev[6], vev[7]]
        pv["vdma"] = p.dma("sync", fm_block_ap(dr["s_v"], 0, 8, tt, 512),
                           sb3(ar.ba[:, 12288:16384], 512), deps=vev)
        for m in range(8):
            for h in range(2):
                slot = (m * 2 + h) % 2
                pairs = [(ar.ba[:, 8192 + i * 1024 + m * 128:
                               8192 + i * 1024 + m * 128 + 128],
                          ar.ba[:, 4096 + i * 1024 + h * 512:
                                4096 + i * 1024 + h * 512 + 512])
                         for i in range(4)]
                mdeps = ev2 + knorm_toks + [kexp_toks[3]]
                if gprev[slot] is not None:
                    mdeps.append(gprev[slot])
                tmm = mm_accum(p, ar.pa[:, 2048 + slot * 512:2048 + slot * 512 + 512],
                               pairs, deps=mdeps)
                tst = p.op("vector",
                           lambda e, m=m, h=h, slot=slot:
                           e.scalar_tensor_tensor(
                               ar.fa[:, m * 1024 + h * 512:
                                     m * 1024 + h * 512 + 512],
                               ar.fa[:, m * 1024 + h * 512:
                                     m * 1024 + h * 512 + 512],
                               ar.ms[:, 56 + m:57 + m],
                               ar.pa[:, 2048 + slot * 512:2048 + slot * 512 + 512],
                               ALU.mult, ALU.add),
                           deps=[tmm, sj_tok0, sj_tok1] + pv["setup"],
                           selfwait=True)
                gprev[slot] = tst
        pv["gmm"] = tmm
        pv["gstt"] = gprev[slot]
        first = False
    p.barrier()


def phase_att_pass2(p, ar, dr, t0, t1):
    oprev = [None, None]
    aprev = [None, None]
    pv = {}
    for ck in range((t1 - t0) // 512):
        tt = t0 + ck * 512
        par = ck % 2
        vb = par * 4096
        ob = 8192 + par * 4096
        xt = ar.fb[:, 0:4096] if par == 0 else ar.fa[:, 0:4096]
        tV = p.dma("sync", sb3(ar.ba[:, vb:vb + 4096], 512),
                   fm_block_ap(dr["s_v"], 0, 8, tt, 512),
                   deps=[pv.get(("v", par))])
        tX = p.dma("sync", sb3(xt, D), stream_chunk_ap(dr["s_att"], tt),
                   deps=[pv.get(("x", par))])
        ofm = []
        for m in range(8):
            slot = m % 2
            pairs = [(ar.bb[:, n * 1024 + m * 128: n * 1024 + m * 128 + 128],
                      ar.ba[:, vb + n * 512: vb + (n + 1) * 512])
                     for n in range(8)]
            mdeps = [tV]
            if oprev[slot] is not None:
                mdeps.append(oprev[slot])
            tmm = mm_accum(p, ar.pa[:, slot * 512:slot * 512 + 512], pairs,
                           deps=mdeps)
            edeps = [tmm]
            if m < 2 and pv.get(("o", par)) is not None:
                edeps.append(pv[("o", par)])
            tev = p.op("scalar",
                       lambda e, m=m, slot=slot, ob=ob: e.copy(
                           ar.ba[:, ob + m * 512: ob + (m + 1) * 512],
                           ar.pa[:, slot * 512:slot * 512 + 512]),
                       deps=edeps, selfwait=False)
            oprev[slot] = tev
            ofm.append(tev)
        adds = []
        for i in range(4):
            for nh in range(2):
                slot = 2 + (i * 2 + nh) % 2
                pairs = [(ar.ba[:, ob + m * 512 + i * 128:
                               ob + m * 512 + i * 128 + 128],
                          ar.wa[:, 24576 + m * 1024 + nh * 512:
                                24576 + m * 1024 + nh * 512 + 512])
                         for m in range(8)]
                pairs.append((ar.ones[0:1, 0:128],
                              ar.br[0:1, 3072 + nh * 512:3072 + nh * 512 + 512]))
                mdeps = ofm + [tX]
                if aprev[slot - 2] is not None:
                    mdeps.append(aprev[slot - 2])
                tmm = mm_accum(p, ar.pa[:, slot * 512:slot * 512 + 512], pairs,
                               deps=mdeps)
                tad = p.op("vector",
                           lambda e, i=i, nh=nh, slot=slot, xt=xt:
                           e.tensor_add(
                               xt[:, i * 1024 + nh * 512:
                                  i * 1024 + nh * 512 + 512],
                               xt[:, i * 1024 + nh * 512:
                                  i * 1024 + nh * 512 + 512],
                               ar.pa[:, slot * 512:slot * 512 + 512]),
                           deps=[tmm], selfwait=True)
                aprev[slot - 2] = tad
                adds.append(tad)
        pv[("o", par)] = tmm
        pv[("x", par)] = p.dma("scalar", stream_chunk_ap(dr["s_att"], tt),
                               sb3(xt, D), deps=adds)
        pv[("v", par)] = tmm
    p.barrier()


def phase_att_gfin(p, ar, dr):
    trc = p.op("vector",
               lambda e: e.reciprocal(ar.fs[:, 24:32], ar.fs[:, 16:24]),
               selfwait=True)
    casts = []
    for m in range(8):
        casts.append(p.op("vector",
                          lambda e, m=m: e.tensor_scalar(
                              ar.ba[:, m * 1024:(m + 1) * 1024],
                              ar.fa[:, m * 1024:(m + 1) * 1024],
                              ar.fs[:, 24 + m:25 + m], None, ALU.mult),
                          deps=[trc], selfwait=True))
    prev = [None, None]
    for m in range(8):
        for n in range(8):
            k = m * 8 + n
            tev = tr128(p, ar,
                        ar.bb[:, n * 1024 + m * 128: n * 1024 + m * 128 + 128],
                        ar.ba[:, m * 1024 + n * 128: m * 1024 + n * 128 + 128],
                        [casts[m]], k % 2, prev[k % 2])
            prev[k % 2] = tev
    p.barrier()


def phase_att_pass2(p, ar, dr, t0, t1):
    for ck in range((t1 - t0) // 512):
        tt = t0 + ck * 512
        tV = p.dma("sync", sb3(ar.ba[:, 0:4096], 512),
                   fm_block_ap(dr["s_v"], 0, 8, tt, 512))
        tX = p.dma("sync", sb3(ar.fb[:, 0:4096], D),
                   stream_chunk_ap(dr["s_att"], tt))
        ofm = []
        oprev = [None, None]
        for m in range(8):
            slot = m % 2
            pairs = [(ar.bb[:, n * 1024 + m * 128: n * 1024 + m * 128 + 128],
                      ar.ba[:, n * 512:(n + 1) * 512]) for n in range(8)]
            mdeps = [tV]
            if oprev[slot] is not None:
                mdeps.append(oprev[slot])
            tmm = mm_accum(p, ar.pa[:, slot * 512:slot * 512 + 512], pairs,
                           deps=mdeps)
            tev = p.op("scalar",
                       lambda e, m=m, slot=slot: e.copy(
                           ar.ba[:, 4096 + m * 512:4096 + (m + 1) * 512],
                           ar.pa[:, slot * 512:slot * 512 + 512]),
                       deps=[tmm], selfwait=False)
            oprev[slot] = tev
            ofm.append(tev)
        adds = []
        aprev = [None, None]
        for i in range(4):
            for nh in range(2):
                slot = 2 + (i * 2 + nh) % 2
                pairs = [(ar.ba[:, 4096 + m * 512 + i * 128:
                               4096 + m * 512 + i * 128 + 128],
                          ar.wa[:, 24576 + m * 1024 + nh * 512:
                                24576 + m * 1024 + nh * 512 + 512])
                         for m in range(8)]
                pairs.append((ar.ones[0:1, 0:128],
                              ar.br[0:1, 3072 + nh * 512:3072 + nh * 512 + 512]))
                mdeps = ofm + [tX]
                if aprev[slot - 2] is not None:
                    mdeps.append(aprev[slot - 2])
                tmm = mm_accum(p, ar.pa[:, slot * 512:slot * 512 + 512], pairs,
                               deps=mdeps)
                tad = p.op("vector",
                           lambda e, i=i, nh=nh, slot=slot: e.tensor_add(
                               ar.fb[:, i * 1024 + nh * 512:
                                     i * 1024 + nh * 512 + 512],
                               ar.fb[:, i * 1024 + nh * 512:
                                     i * 1024 + nh * 512 + 512],
                               ar.pa[:, slot * 512:slot * 512 + 512]),
                           deps=[tmm], selfwait=True)
                aprev[slot - 2] = tad
                adds.append(tad)
        p.dma("scalar", stream_chunk_ap(dr["s_att"], tt),
              sb3(ar.fb[:, 0:4096], D), deps=adds)
        p.barrier()


def phase_att_layer(p, ar, dr, l, t0, t1):
    sdeps = phase_att_setup(p, ar, dr, l)
    phase_att_pass1(p, ar, dr, t0, t1, setup_deps=sdeps)
    phase_att_gfin(p, ar, dr)
    phase_att_pass2(p, ar, dr, t0, t1)



def phase_make_fm(p, ar, dr, t0, t1, src="s_f", dst="s_fm"):
    for ck in range((t1 - t0) // 512):
        tt = t0 + ck * 512
        tX = p.dma("gpsimd", sb3(ar.ba[:, 0:4096], D),
                   stream_chunk_ap(dr[src], tt))
        ev = []
        prev = [None, None]
        for i in range(4):
            for j in range(8):
                k = i * 8 + j
                tev = tr128(p, ar,
                            ar.bb[:, j * 512 + i * 128: j * 512 + i * 128 + 128],
                            ar.ba[:, i * 1024 + j * 128: i * 1024 + j * 128 + 128],
                            [tX], k % 2, prev[k % 2])
                prev[k % 2] = tev
                ev.append(tev)
        p.dma("sync", fm_block_ap(dr[dst], 0, 8, tt, 512),
              sb3(ar.bb[:, 0:4096], 512), deps=ev)
        p.barrier()


def phase_moe_gate(p, ar, dr, t0, t1, E, topk, gw_key, gb_key=None,
                   s_fm="s_fm", s_wd="s_wd"):
    tw = p.dma("gpsimd", sb3(ar.wa[:, 0:8 * E], E),
               dr[gw_key].rearrange("(c p) e -> p c e", p=128))
    p.op("vector", lambda e: e.memset(ar.ms[:, 104:168], 0.0))
    tgb = None
    if gb_key is not None:
        tgb = p.dma("gpsimd", ar.br[0:1, 0:E], row2d(dr[gb_key]))
    p.barrier()
    for ck in range((t1 - t0) // 512):
        tt = t0 + ck * 512
        tX = p.dma("sync", sb3(ar.bb[:, 0:4096], 512),
                   fm_block_ap(dr[s_fm], 0, 8, tt, 512))
        wtoks = []
        eprev = None
        for i in range(4):
            pairs = [(ar.bb[:, jk * 512 + i * 128: jk * 512 + i * 128 + 128],
                      ar.wa[:, jk * E:(jk + 1) * E]) for jk in range(8)]
            if gb_key is not None:
                pairs.append((ar.ones[0:1, 0:128], ar.br[0:1, 0:E]))
            mdeps = [tX]
            if eprev is not None:
                mdeps.append(eprev)
            tmm = mm_accum(p, ar.pa[:, 0:E], pairs, deps=mdeps)
            c = 0
            tmx = p.op("vector",
                       lambda e, i=i: e.reduce_max(
                           ar.ms[:, 0:1], ar.pa[:, 0:E], axis=AX.X,
                           negate=True),
                       deps=[tmm], selfwait=True)
            tex = p.op("scalar",
                       lambda e, i=i: e.activation(
                           ar.ms[:, 8:8 + E], ar.pa[:, 0:E], AF.Exp,
                           bias=ar.ms[:, 0:1], accum_out=ar.ms[:, 1:2]),
                       deps=[tmx], selfwait=True)
            eprev = tex
            trc = p.op("vector",
                       lambda e: e.reciprocal(ar.ms[:, 2:3], ar.ms[:, 1:2]),
                       deps=[tex], selfwait=True)
            tsm = p.op("vector",
                       lambda e: e.tensor_scalar(
                           ar.ms[:, 8:8 + E], ar.ms[:, 8:8 + E],
                           ar.ms[:, 2:3], None, ALU.mult),
                       deps=[trc], selfwait=True)
            tprev = tsm
            tcur = p.op("vector",
                        lambda e: e.tensor_copy(ar.ms[:, 32:32 + E],
                                                ar.ms[:, 8:8 + E]),
                        deps=[tsm], selfwait=True)
            for r in range(topk - 1):
                tmx2 = p.op("vector",
                            lambda e: e.reduce_max(ar.ms[:, 3:4],
                                                   ar.ms[:, 32:32 + E],
                                                   axis=AX.X),
                            deps=[tcur], selfwait=True)
                tmsk = p.op("vector",
                            lambda e: e.tensor_scalar(
                                ar.ms[:, 56:56 + E], ar.ms[:, 32:32 + E],
                                ar.ms[:, 3:4], None, ALU.is_equal),
                            deps=[tmx2], selfwait=True)
                tcur = p.op("vector",
                            lambda e: e.scalar_tensor_tensor(
                                ar.ms[:, 32:32 + E], ar.ms[:, 56:56 + E],
                                -1e30, ar.ms[:, 32:32 + E],
                                ALU.mult, ALU.add),
                            deps=[tmsk], selfwait=True)
            tthr = p.op("vector",
                        lambda e: e.reduce_max(ar.ms[:, 4:5],
                                               ar.ms[:, 32:32 + E],
                                               axis=AX.X),
                        deps=[tcur], selfwait=True)
            tm2 = p.op("vector",
                       lambda e: e.tensor_scalar(
                           ar.ms[:, 56:56 + E], ar.ms[:, 8:8 + E],
                           ar.ms[:, 4:5], None, ALU.is_ge),
                       deps=[tthr], selfwait=True)
            twd = p.op("vector",
                       lambda e, i=i: e.tensor_mul(
                           ar.ms[:, 104 + i * 16:104 + i * 16 + E],
                           ar.ms[:, 8:8 + E], ar.ms[:, 56:56 + E]),
                       deps=[tm2], selfwait=True)
            wtoks.append(twd)
        pvfm["eprev"] = eprev
        pvfm["wdst"] = p.dma(
            "scalar",
            dr[s_wd][tt:tt + 512, 0:16].rearrange("(n p) e -> p n e", p=128),
            sb3(ar.ms[:, 104:168], 16), deps=wtoks)
    p.barrier()


def phase_moe_expert(p, ar, dr, t0, t1, *, w1_ap, w2_ap, hdim, gated,
                     b1_ap=None, b2_ap=None, wd_col=None, z_first=False,
                     s_fm="s_fm", s_z="s_z"):
    actdim = hdim // 2 if gated else hdim
    nh = hdim // 128
    na = actdim // 128
    w1toks = load_wT_bf16(p, ar.wa, w1_ap, D, hdim, col0=0)
    w2toks = load_wT_bf16(p, ar.wa, w2_ap, actdim, D, col0=8 * hdim)
    brow_toks = []
    if b1_ap is not None:
        brow_toks.append(p.dma("gpsimd", ar.br[0:1, 0:hdim], row2d(b1_ap)))
    if b2_ap is not None:
        brow_toks.append(p.dma("gpsimd", ar.br[0:1, 2048:3072], row2d(b2_ap)))
    xfm_last = [None, None]
    wd_last = [None, None]
    eo_last = [None, None]
    nck = (t1 - t0) // 512
    for ck in range(nck):
        tt = t0 + ck * 512
        par = ck % 2
        xb = 4096 * par
        tX = p.dma("sync", sb3(ar.bb[:, xb:xb + 4096], 512),
                   fm_block_ap(dr[s_fm], 0, 8, tt, 512),
                   deps=[xfm_last[par]])
        if wd_col is not None:
            wb = 104 + 64 * par
            tW = p.dma("sync", sb3(ar.ms[:, wb:wb + 64], 16),
                       dr["s_wd"][tt:tt + 512, 0:16].rearrange(
                           "(n p) e -> p n e", p=128),
                       deps=[wd_last[par]])
        hev = [None] * nh
        hmm_last = None
        for j in range(nh):
            slot = j % 2
            pairs = [(ar.wa[:, jk * hdim + j * 128: jk * hdim + j * 128 + 128],
                      ar.bb[:, xb + jk * 512: xb + (jk + 1) * 512])
                     for jk in range(8)]
            if b1_ap is not None:
                pairs.append((ar.br[0:1, j * 128:j * 128 + 128],
                              ar.ones[0:1, 0:512]))
            mdeps = [tX] + w1toks + brow_toks
            if j >= 2:
                mdeps.append(hev[j - 2])
            tmm = mm_accum(p, ar.pa[:, slot * 512:slot * 512 + 512], pairs,
                           deps=mdeps)
            hmm_last = tmm
            if gated:
                if j < nh // 2:
                    tev = p.op("scalar",
                               lambda e, j=j, slot=slot: e.copy(
                                   ar.ba[:, j * 512:(j + 1) * 512],
                                   ar.pa[:, slot * 512:slot * 512 + 512]),
                               deps=[tmm], selfwait=False)
                else:
                    jj = j - nh // 2
                    tev = p.op("scalar",
                               lambda e, jj=jj, slot=slot: e.activation(
                                   ar.ba[:, 4096 + jj * 512:
                                         4096 + (jj + 1) * 512],
                                   ar.pa[:, slot * 512:slot * 512 + 512],
                                   AF.Silu),
                               deps=[tmm], selfwait=False)
            else:
                tev = p.op("scalar",
                           lambda e, j=j, slot=slot: e.activation(
                               ar.ba[:, 8192 + j * 512:8192 + (j + 1) * 512],
                               ar.pa[:, slot * 512:slot * 512 + 512],
                               AF.Gelu),
                           deps=[tmm], selfwait=False)
            hev[j] = tev
        xfm_last[par] = hmm_last
        if gated:
            atoks = []
            for j in range(na):
                atoks.append(p.op(
                    "vector",
                    lambda e, j=j: e.tensor_mul(
                        ar.ba[:, 8192 + j * 512:8192 + (j + 1) * 512],
                        ar.ba[:, j * 512:(j + 1) * 512],
                        ar.ba[:, 4096 + j * 512:4096 + (j + 1) * 512]),
                    deps=[hev[j], hev[j + nh // 2]], selfwait=False))
        else:
            atoks = hev
        eost = ar.fb if par == 0 else ar.fa
        etoks = []
        eprev = [None, None]
        for i in range(4):
            for nhf in range(2):
                slot = 2 + (i * 2 + nhf) % 2
                pairs = [(ar.ba[:, 8192 + m * 512 + i * 128:
                               8192 + m * 512 + i * 128 + 128],
                          ar.wa[:, 8 * hdim + m * 1024 + nhf * 512:
                                8 * hdim + m * 1024 + nhf * 512 + 512])
                         for m in range(na)]
                if b2_ap is not None:
                    pairs.append((ar.ones[0:1, 0:128],
                                  ar.br[0:1, 2048 + nhf * 512:
                                        2048 + nhf * 512 + 512]))
                mdeps = list(atoks) + w2toks
                if eprev[slot - 2] is not None:
                    mdeps.append(eprev[slot - 2])
                tmm = mm_accum(p, ar.pa[:, slot * 512:slot * 512 + 512],
                               pairs, deps=mdeps)
                vdeps = [tmm, eo_last[par]]
                if wd_col is not None:
                    vdeps.append(tW)
                    tev = p.op("vector",
                               lambda e, i=i, nhf=nhf, slot=slot, wb=wb,
                               eost=eost:
                               e.tensor_scalar(
                                   eost[:, i * 1024 + nhf * 512:
                                        i * 1024 + nhf * 512 + 512],
                                   ar.pa[:, slot * 512:slot * 512 + 512],
                                   ar.ms[:, wb + i * 16 + wd_col:
                                         wb + i * 16 + wd_col + 1],
                                   None, ALU.mult),
                               deps=vdeps, selfwait=False)
                else:
                    tev = p.op("vector",
                               lambda e, i=i, nhf=nhf, slot=slot, eost=eost:
                               e.tensor_copy(
                                   eost[:, i * 1024 + nhf * 512:
                                        i * 1024 + nhf * 512 + 512],
                                   ar.pa[:, slot * 512:slot * 512 + 512]),
                               deps=vdeps, selfwait=False)
                eprev[slot - 2] = tev
                etoks.append(tev)
        tz = p.dma("gpsimd", stream_chunk_ap(dr[s_z], tt),
                   sb3(eost[:, 0:4096], D),
                   deps=etoks, accum=not z_first)
        eo_last[par] = tz
        if wd_col is not None:
            wd_last[par] = etoks[-1]
    p.barrier()


def phase_copy_stream(p, ar, dr, t0, t1, src, dst):
    prev = None
    for ck in range((t1 - t0) // 512):
        tt = t0 + ck * 512
        half = (ck % 2) * 4096
        t = p.dma("sync", sb3(ar.fa[:, half:half + 4096], D),
                  stream_chunk_ap(dr[src], tt),
                  deps=[prev] if prev else ())
        prev = p.dma("scalar", stream_chunk_ap(dr[dst], tt),
                     sb3(ar.fa[:, half:half + 4096], D), deps=[t])
    p.barrier()



def phase_ffn0(p, ar, dr, t0, t1):
    phase_make_fm_gate(p, ar, dr, t0, t1, E=8, topk=2, gw_key="gate_wT")
    for e in range(8):
        phase_moe_expert(
            p, ar, dr, t0, t1,
            w1_ap=dr["W1Tf"][e * 1024:(e + 1) * 1024, :],
            w2_ap=dr["W2Tf"][e * 768:(e + 1) * 768, :],
            hdim=1536, gated=True, wd_col=e, z_first=(e == 0))
    for s in range(2):
        phase_moe_expert(
            p, ar, dr, t0, t1,
            w1_ap=dr["Ws1P"][:, s * 1536:(s + 1) * 1536],
            w2_ap=dr["Ws2P"][s * 768:(s + 1) * 768, :],
            hdim=1536, gated=True, wd_col=None, z_first=False)



def phase_ffn1(p, ar, dr, t0, t1):
    phase_make_fm_gate(p, ar, dr, t0, t1, E=15, topk=3, gw_key="gWT",
                       gb_key="gb")
    phase_copy_stream(p, ar, dr, t0, t1, "s_f", "s_z")
    for s in range(4):
        phase_moe_expert(
            p, ar, dr, t0, t1,
            w1_ap=dr["sW1m"][:, s * 1024:(s + 1) * 1024],
            w2_ap=dr["sW2m"][s * 1024:(s + 1) * 1024, :],
            hdim=1024, gated=False,
            b1_ap=dr["sb1"][s * 1024:(s + 1) * 1024],
            b2_ap=dr["sb2"] if s == 0 else None,
            wd_col=None, z_first=False)
    for e in range(15):
        phase_moe_expert(
            p, ar, dr, t0, t1,
            w1_ap=dr["rW1f"][e * 1024:(e + 1) * 1024, :],
            w2_ap=dr["rW2f"][e * 2048:(e + 1) * 2048, :],
            hdim=2048, gated=False,
            b1_ap=dr["rb1f"][e * 2048:(e + 1) * 2048],
            b2_ap=dr["rb2f"][e * 1024:(e + 1) * 1024],
            wd_col=e, z_first=False)


def tr128f(p, ar, dst, src, deps, slot, prev_evac, evac="vector"):
    tdeps = list(deps)
    if prev_evac is not None:
        tdeps.append(prev_evac)
    t1 = p.op("tensor",
              lambda e, dst=dst, src=src, slot=slot:
              e.transpose(ar.pa[:, 2048 + slot * 512:2048 + slot * 512 + 128],
                          src, ar.eyef[:, :]),
              deps=tdeps, selfwait=False)
    t2 = p.op(evac,
              lambda e, dst=dst, slot=slot:
              e.tensor_copy(dst, ar.pa[:, 2048 + slot * 512:2048 + slot * 512 + 128]),
              deps=[t1], selfwait=False)
    return t2


def phase_make_fm_gate(p, ar, dr, t0, t1, E=None, topk=None, gw_key=None,
                       gb_key=None, src="s_f", dst="s_fm", s_wd="s_wd"):
    if E is not None:
        tw = p.dma("sync", sb3(ar.fs[:, 256:256 + 8 * 16], 16)[:, :, 0:E],
                   dr[gw_key].rearrange("(c p) e -> p c e", p=128))
        if gb_key is not None:
            p.dma("sync", ar.brf[0:1, 0:E], row2d(dr[gb_key]))
        p.op("vector", lambda e: e.memset(ar.ms[:, 104:168], 0.0))
        p.op("vector", lambda e: e.memset(ar.fs[0:1, 384:512], 1.0))
        p.barrier()
    pvfm = {}
    for ck in range((t1 - t0) // 512):
        tt = t0 + ck * 512
        par = ck % 2
        ab = par * 4096
        tX = p.dma("gpsimd", sb3(ar.ba[:, ab:ab + 4096], D),
                   stream_chunk_ap(dr[src], tt),
                   deps=[pvfm.get(("tr", par))])
        tXf = p.dma("sync", sb3(ar.fa[:, ab:ab + 4096], D),
                    stream_chunk_ap(dr[src], tt),
                    deps=[pvfm.get(("trf", par))]) if E is not None else None
        ev = []
        prev = [None, None]
        for i in range(4):
            for j in range(8):
                k = i * 8 + j
                tdeps = [tX]
                if k < 2 and pvfm.get(("fmdma", par)) is not None:
                    tdeps.append(pvfm[("fmdma", par)])
                tev = tr128(p, ar,
                            ar.bb[:, ab + j * 512 + i * 128:
                                  ab + j * 512 + i * 128 + 128],
                            ar.ba[:, ab + i * 1024 + j * 128:
                                  ab + i * 1024 + j * 128 + 128],
                            tdeps, k % 2, prev[k % 2],
                            evac="scalar" if k % 2 else "vector")
                prev[k % 2] = tev
                ev.append(tev)
        pvfm[("tr", par)] = ev[-1]
        pvfm[("fmdma", par)] = p.dma(
            "sync", fm_block_ap(dr[dst], 0, 8, tt, 512),
            sb3(ar.bb[:, ab:ab + 4096], 512), deps=ev)
        if E is None:
            continue
        evf = []
        prevf = [None, None]
        for i in range(4):
            for j in range(8):
                k = i * 8 + j
                tdeps = [tXf]
                if k < 2 and pvfm.get("gmm") is not None:
                    tdeps.append(pvfm["gmm"])
                tev = tr128f(p, ar,
                             ar.fb[:, j * 512 + i * 128: j * 512 + i * 128 + 128],
                             ar.fa[:, ab + i * 1024 + j * 128:
                                   ab + i * 1024 + j * 128 + 128],
                             tdeps, k % 2, prevf[k % 2])
                prevf[k % 2] = tev
                evf.append(tev)
        pvfm[("trf", par)] = evf[-1]
        wtoks = []
        eprev = pvfm.get("eprev")
        for i in range(4):
            pairs = [(ar.fb[:, jk * 512 + i * 128: jk * 512 + i * 128 + 128],
                      ar.fs[:, 256 + jk * 16:256 + jk * 16 + E])
                     for jk in range(8)]
            if gb_key is not None:
                pairs.append((ar.fs[0:1, 384:512], ar.brf[0:1, 0:E]))
            mdeps = list(evf)
            if eprev is not None:
                mdeps.append(eprev)
            tmm = mm_accum(p, ar.pa[:, 0:E], pairs, deps=mdeps)
            pvfm["gmm"] = tmm
            tmx = p.op("vector",
                       lambda e: e.reduce_max(ar.ms[:, 0:1], ar.pa[:, 0:E],
                                              axis=AX.X, negate=True),
                       deps=[tmm] + ([pvfm.get("wdst")] if i == 0 else []),
                       selfwait=True)
            tex = p.op("scalar",
                       lambda e: e.activation(ar.ms[:, 8:8 + E],
                                              ar.pa[:, 0:E], AF.Exp,
                                              bias=ar.ms[:, 0:1],
                                              accum_out=ar.ms[:, 1:2]),
                       deps=[tmx], selfwait=True)
            eprev = tex
            trc = p.op("vector",
                       lambda e: e.reciprocal(ar.ms[:, 2:3], ar.ms[:, 1:2]),
                       deps=[tex], selfwait=True)
            tsm = p.op("vector",
                       lambda e: e.tensor_scalar(ar.ms[:, 8:8 + E],
                                                 ar.ms[:, 8:8 + E],
                                                 ar.ms[:, 2:3], None, ALU.mult),
                       deps=[trc], selfwait=True)
            tcur = p.op("vector",
                        lambda e: e.tensor_copy(ar.ms[:, 32:32 + E],
                                                ar.ms[:, 8:8 + E]),
                        deps=[tsm], selfwait=True)
            for r in range(topk - 1):
                tmx2 = p.op("vector",
                            lambda e: e.reduce_max(ar.ms[:, 3:4],
                                                   ar.ms[:, 32:32 + E],
                                                   axis=AX.X),
                            deps=[tcur], selfwait=True)
                tmsk = p.op("vector",
                            lambda e: e.tensor_scalar(
                                ar.ms[:, 56:56 + E], ar.ms[:, 32:32 + E],
                                ar.ms[:, 3:4], None, ALU.is_equal),
                            deps=[tmx2], selfwait=True)
                tcur = p.op("vector",
                            lambda e: e.scalar_tensor_tensor(
                                ar.ms[:, 32:32 + E], ar.ms[:, 56:56 + E],
                                -1e30, ar.ms[:, 32:32 + E],
                                ALU.mult, ALU.add),
                            deps=[tmsk], selfwait=True)
            tthr = p.op("vector",
                        lambda e: e.reduce_max(ar.ms[:, 4:5],
                                               ar.ms[:, 32:32 + E], axis=AX.X),
                        deps=[tcur], selfwait=True)
            tm2 = p.op("vector",
                       lambda e: e.tensor_scalar(ar.ms[:, 56:56 + E],
                                                 ar.ms[:, 8:8 + E],
                                                 ar.ms[:, 4:5], None, ALU.is_ge),
                       deps=[tthr], selfwait=True)
            twd = p.op("vector",
                       lambda e, i=i: e.tensor_mul(
                           ar.ms[:, 104 + i * 16:104 + i * 16 + E],
                           ar.ms[:, 8:8 + E], ar.ms[:, 56:56 + E]),
                       deps=[tm2], selfwait=True)
            wtoks.append(twd)
        pvfm["eprev"] = eprev
        pvfm["wdst"] = p.dma(
            "scalar",
            dr[s_wd][tt:tt + 512, 0:16].rearrange("(n p) e -> p n e", p=128),
            sb3(ar.ms[:, 104:168], 16), deps=wtoks)
    p.barrier()



def phase_conv(p, ar, dr, t0, t1):
    tw = load_wT_bf16(p, ar.wa, dr["x_projT"], 4096, 96, col0=0)
    tcw = p.dma("sync", ar.fs[:, 0:128], dr["conv_wc"])
    tcb = p.dma("sync", ar.fs[:, 128:160], dr["conv_bc"])
    p.barrier()
    nt = (t1 - t0) // 2048
    for ck in range(nt):
        tt = t0 + ck * 2048
        xev = [None, None, None, None]
        prev_t3 = None
        prev_d1 = None
        prev_mm = []
        prev_d2 = None
        for j in range(32):
            fdeps = [prev_t3] if prev_t3 else []
            if tt == 0:
                tz = p.op("vector",
                          lambda e, j=j: e.memset(ar.fa[:, 0:3], 0.0),
                          deps=fdeps, selfwait=False)
                tl = p.dma("sync", ar.fa[:, 3:2051],
                           dr["s_uraw"][j * 128:(j + 1) * 128, tt:tt + 2048],
                           deps=fdeps + [tz])
                ldeps = [tl, tz]
            else:
                tl = p.dma("sync", ar.fa[:, 0:2051],
                           dr["s_uraw"][j * 128:(j + 1) * 128, tt - 3:tt + 2048],
                           deps=fdeps)
                ldeps = [tl]
                if tt == T_HALF and "cmask128" in dr:
                    tmsk = p.op("vector",
                                lambda e: e.tensor_scalar(
                                    ar.fa[:, 0:3], ar.fa[:, 0:3],
                                    ar.cf[:, 2:3], None, ALU.mult),
                                deps=[tl], selfwait=False)
                    ldeps = [tl, tmsk]
            t0_ = p.op("vector",
                       lambda e, j=j: e.tensor_scalar(
                           ar.fb[:, 0:2048], ar.fa[:, 0:2048],
                           ar.fs[:, j * 4:j * 4 + 1],
                           ar.fs[:, 128 + j:129 + j],
                           ALU.mult, ALU.add),
                       deps=ldeps, selfwait=False)
            t1_ = p.op("vector",
                       lambda e, j=j: e.scalar_tensor_tensor(
                           ar.fb[:, 2048:4096], ar.fa[:, 1:2049],
                           ar.fs[:, j * 4 + 1:j * 4 + 2],
                           ar.fb[:, 0:2048], ALU.mult, ALU.add),
                       deps=[t0_], selfwait=True)
            t2_ = p.op("vector",
                       lambda e, j=j: e.scalar_tensor_tensor(
                           ar.fb[:, 0:2048], ar.fa[:, 2:2050],
                           ar.fs[:, j * 4 + 2:j * 4 + 3],
                           ar.fb[:, 2048:4096], ALU.mult, ALU.add),
                       deps=[t1_], selfwait=True)
            t3_ = p.op("vector",
                       lambda e, j=j: e.scalar_tensor_tensor(
                           ar.fb[:, 2048:4096], ar.fa[:, 3:2051],
                           ar.fs[:, j * 4 + 3:j * 4 + 4],
                           ar.fb[:, 0:2048], ALU.mult, ALU.add),
                       deps=[t2_], selfwait=True)
            tsil = p.op("scalar",
                        lambda e: e.activation(ar.fa[:, 2176:4224],
                                               ar.fb[:, 2048:4096], AF.Silu),
                        deps=[t3_] + ([prev_d1] if prev_d1 else []),
                        selfwait=False)
            tcast = p.op("vector",
                         lambda e: e.tensor_copy(ar.bb[:, 0:2048],
                                                 ar.fa[:, 2176:4224]),
                         deps=[tsil] + prev_mm +
                         ([prev_d2] if prev_d2 else []), selfwait=True)
            d1 = p.dma("scalar", dr["s_uc"][j * 128:(j + 1) * 128, tt:tt + 2048],
                       ar.fa[:, 2176:4224], deps=[tsil])
            d2 = p.dma("gpsimd", dr["s_ucb"][j * 128:(j + 1) * 128, tt:tt + 2048],
                       ar.bb[:, 0:2048], deps=[tcast])
            prev_mm = []
            for i in range(4):
                tmmm = p.op(
                    "tensor",
                    lambda e, j=j, i=i: e.matmul(
                        ar.pa[0:96, i * 512:(i + 1) * 512],
                        ar.wa[:, j * 96:(j + 1) * 96],
                        ar.bb[:, i * 512:(i + 1) * 512],
                        start=(j == 0), stop=(j == 31)),
                    deps=[tcast], selfwait=False)
                xev[i] = tmmm
                prev_mm.append(tmmm)
            prev_t3 = t3_
            prev_d1 = d1
            prev_d2 = d2
        etoks = []
        for i in range(4):
            etoks.append(p.op("vector",
                              lambda e, i=i: e.tensor_copy(
                                  ar.fs[0:96, 1024 + i * 512:1024 + (i + 1) * 512],
                                  ar.pa[0:96, i * 512:(i + 1) * 512]),
                              deps=[xev[i]], selfwait=False))
        p.dma("sync", dr["s_xdbl"][0:96, tt:tt + 2048],
              ar.fs[0:96, 1024:3072], deps=etoks)
        p.barrier()



def phase_dt(p, ar, dr, t0, t1):
    tw = p.dma("gpsimd", ar.wa[0:64, 0:4096], dr["dt_projT"])
    tb = p.dma("gpsimd", ar.br[0:1, 0:4096], row2d(dr["dt_proj_b"]))
    p.barrier()
    nt = (t1 - t0) // 2048
    UB = [0, 4096]
    DB = [2048, 6144]
    SB = [0, 2048]
    last_nd = [None, None]
    last_d1 = [None, None]
    last_d2 = [None, None]
    last_sg = [None, None]
    prev_ln = [None, None]
    for ck in range(nt):
        tt = t0 + ck * 2048
        tdi = p.dma("gpsimd", ar.bb[0:64, 2048 * (ck % 2):2048 * (ck % 2) + 2048],
                    dr["s_xdbl"][0:64, tt:tt + 2048],
                    deps=[last_nd[0], last_nd[1]] if ck else ())
        xb = 2048 * (ck % 2)
        for j in range(32):
            par = j % 2
            tu = p.dma("sync", ar.fa[:, UB[par]:UB[par] + 2048],
                       dr["s_uc"][j * 128:(j + 1) * 128, tt:tt + 2048],
                       deps=[last_nd[par]])
            mms = []
            for i in range(4):
                slot = i % 2
                pairs = [(ar.wa[0:64, j * 128:(j + 1) * 128],
                          ar.bb[0:64, xb + i * 512: xb + (i + 1) * 512]),
                         (ar.br[0:1, j * 128:j * 128 + 128],
                          ar.ones[0:1, 0:512])]
                mdeps = [tdi, tw, tb]
                if len(mms) >= 2:
                    mdeps.append(mms[i - 2])
                elif last_sg[slot] is not None:
                    mdeps.append(last_sg[slot])
                tmm = mm_accum(p, ar.pa[:, slot * 512:slot * 512 + 512], pairs,
                               deps=mdeps)
                tsg = p.op("scalar",
                           lambda e, i=i, slot=slot, par=par: e.activation(
                               ar.fb[:, SB[par] + i * 512:SB[par] + (i + 1) * 512],
                               ar.pa[:, slot * 512:slot * 512 + 512],
                               AF.Sigmoid, scale=-1.0),
                           deps=[tmm] + ([last_d2[par], last_nd[par]]
                                         if i == 0 else []),
                           selfwait=False)
                mms.append(tsg)
                last_sg[slot] = tsg
            tln = [p.op("scalar",
                        lambda e, i=i, par=par: e.activation(
                            ar.fa[:, DB[par] + i * 512:DB[par] + (i + 1) * 512],
                            ar.fb[:, SB[par] + i * 512:SB[par] + (i + 1) * 512],
                            AF.Ln),
                        deps=[mms[i]] + ([last_d1[par]] if i == 0 else []),
                        selfwait=True) for i in range(4)]
            tcl = p.op("vector",
                       lambda e, par=par: e.tensor_scalar(
                           ar.fa[:, DB[par]:DB[par] + 2048],
                           ar.fa[:, DB[par]:DB[par] + 2048],
                           -87.0, None, ALU.max),
                       deps=tln, selfwait=False)
            tnd = p.op("vector",
                       lambda e, par=par: e.tensor_mul(
                           ar.bb[:, 4096 + par * 2048:4096 + par * 2048 + 2048],
                           ar.fa[:, DB[par]:DB[par] + 2048],
                           ar.fa[:, UB[par]:UB[par] + 2048]),
                       deps=[tcl, tu] + tln, selfwait=True)
            last_nd[par] = tnd
            last_d1[par] = p.dma("scalar",
                                 dr["s_dtm"][j * 128:(j + 1) * 128, tt:tt + 2048],
                                 ar.fa[:, DB[par]:DB[par] + 2048], deps=[tcl])
            last_d2[par] = p.dma("sync",
                                 dr["s_ndtu"][j * 128:(j + 1) * 128, tt:tt + 2048],
                                 ar.bb[:, 4096 + par * 2048:4096 + par * 2048 + 2048],
                                 deps=[tnd])
    p.barrier()


def phase_scan_setup(p, ar, dr):
    ta_ = p.dma("sync", ar.fs[:, 1024:1536], dr["Alogc"])
    p.op("scalar", lambda e: e.activation(ar.fs[:, 512:1024],
                                          ar.fs[:, 1024:1536], AF.Exp),
         deps=[ta_], selfwait=False)
    p.dma("gpsimd", ar.wa[0:32, 0:4096], dr["sel16"])
    p.op("vector", lambda e: e.memset(ar.fs[:, 0:512], 0.0))
    p.barrier()


def phase_scan(p, ar, dr, ta, tb, yout=True, bscale_ap=None):
    nt = (tb - ta) // 1024
    lane_tail = []
    prev_bcb = None
    for ck in range(nt):
        tt = ta + ck * 1024
        tbc = p.dma("sync", ar.fs[0:32, 1024:2048],
                    dr["s_xdbl"][64:96, tt:tt + 1024], deps=[prev_bcb])
        tbcb = p.op("vector",
                    lambda e: e.tensor_copy(ar.bb[0:32, 0:1024],
                                            ar.fs[0:32, 1024:2048]),
                    deps=[tbc] + lane_tail, selfwait=True)
        prev_bcb = tbcb
        btoks = {}
        bprev = list(lane_tail) + [None, None]
        bprev = bprev[-2:] if len(bprev) > 2 else [None, None]
        for s in range(16):
            for half in range(2):
                tB = p.op("tensor",
                          lambda e, s=s, half=half: e.matmul(
                              ar.pa[:, 2048:2560],
                              ar.wa[0:32, s * 128:(s + 1) * 128],
                              ar.bb[0:32, half * 512:half * 512 + 512],
                              start=True, stop=True),
                          deps=[tbcb, bprev[0]] + lane_tail,
                          selfwait=False)
                tC = p.op("tensor",
                          lambda e, s=s, half=half: e.matmul(
                              ar.pa[:, 2560:3072],
                              ar.wa[0:32, 2048 + s * 128:
                                    2048 + s * 128 + 128],
                              ar.bb[0:32, half * 512:half * 512 + 512],
                              start=True, stop=True),
                          deps=[bprev[1]], selfwait=False)
                eB = p.op("vector",
                          lambda e, s=s, half=half: e.tensor_scalar(
                              ar.ba[:, s * 1024 + half * 512:
                                    s * 1024 + half * 512 + 512],
                              ar.pa[:, 2048:2560],
                              -1.0 if bscale_ap is None else bscale_ap,
                              None, ALU.mult),
                          deps=[tB], selfwait=False)
                eC = p.op("vector",
                          lambda e, s=s, half=half: e.tensor_copy(
                              ar.wa[:, 4096 + s * 1024 + half * 512:
                                    4096 + s * 1024 + half * 512 + 512],
                              ar.pa[:, 2560:3072]),
                          deps=[tC], selfwait=False)
                bprev = [eB, eC]
                btoks[(s, half)] = (eB, eC)
        ball = [t for pair in btoks.values() for t in pair]
        if ck == 0:
            prev_j_v = None
            prev_j_g = None
        for jp in range(16):
            jv, jg = 2 * jp, 2 * jp + 1
            tdm_v = p.dma("sync", ar.fa[:, 0:1024],
                          dr["s_dtm"][jv * 128:(jv + 1) * 128, tt:tt + 1024],
                          deps=[prev_j_v])
            tdu_v = p.dma("scalar", ar.bb[:, 1024:2048],
                          dr["s_ndtu"][jv * 128:(jv + 1) * 128, tt:tt + 1024],
                          deps=[prev_j_v])
            tdm_g = p.dma("sync", ar.fa[:, 1024:2048],
                          dr["s_dtm"][jg * 128:(jg + 1) * 128, tt:tt + 1024],
                          deps=[prev_j_g])
            tdu_g = p.dma("scalar", ar.bb[:, 2048:3072],
                          dr["s_ndtu"][jg * 128:(jg + 1) * 128, tt:tt + 1024],
                          deps=[prev_j_g])
            st_v = {"h": None, "car": None, "scan": None, "scan2": None,
                    "y": None}
            st_g = {"h": None, "car": None, "scan": None, "scan2": None,
                    "y": None}
            for s in range(16):
                for gl in (False, True):
                    st = st_g if gl else st_v
                    j = jg if gl else jv
                    col = j * 16 + s
                    lane = "gpsimd" if gl else "vector"
                    if gl:
                        da_ap = ar.fb[:, 2048:3072]
                        edeps = [tdm_g, st["scan"]]
                        b_ap = ar.bb[:, 4096:5120]
                        dut = ar.bb[:, 2048:3072]
                        dmt = ar.fa[:, 1024:2048]
                        HH, HC, YY = 5120, 6144, 7168
                        tdu_, prev_j = tdu_g, prev_j_g
                    else:
                        da_ap = ar.pa[:, (s % 2) * 1024:(s % 2) * 1024 + 1024]
                        edeps = [tdm_v, st["scan2"]]
                        b_ap = ar.bb[:, 3072:4096]
                        dut = ar.bb[:, 1024:2048]
                        dmt = ar.fa[:, 0:1024]
                        HH, HC, YY = 2048, 3072, 4096
                        tdu_, prev_j = tdu_v, prev_j_v
                    tda = p.op("scalar",
                               lambda e, col=col, da_ap=da_ap, dmt=dmt:
                               e.activation(da_ap, dmt, AF.Exp,
                                            scale=ar.fs[:, 512 + col:513 + col]),
                               deps=edeps, selfwait=True)
                    tb_ = p.op(lane,
                               lambda e, s=s, b_ap=b_ap, dut=dut:
                               e.tensor_mul(b_ap, dut,
                                            ar.ba[:, s * 1024:(s + 1) * 1024]),
                               deps=[tdu_] + (ball if (jp == 0 and s == 0)
                                              else []),
                               selfwait=True)
                    tscan = p.op("vector",
                                 lambda e, col=col, da_ap=da_ap, b_ap=b_ap,
                                 HH=HH:
                                 e.tensor_tensor_scan(
                                     ar.fa[:, HH:HH + 1024], da_ap, b_ap,
                                     ar.fs[:, col:col + 1],
                                     ALU.mult, ALU.add),
                                 deps=[tb_, tda, st["h"], st["car"]],
                                 selfwait=True)
                    st["scan2"] = st["scan"]
                    st["scan"] = tscan
                    tcar = p.op("gpsimd",
                                lambda e, col=col, HH=HH: e.tensor_copy(
                                    ar.fs[:, col:col + 1],
                                    ar.fa[:, HH + 1023:HH + 1024]),
                                deps=[tscan], selfwait=True)
                    st["car"] = tcar
                    if yout:
                        if s == 0:
                            thc = p.op(lane,
                                       lambda e, s=s, HH=HH, YY=YY:
                                       e.tensor_mul(
                                           ar.fa[:, YY:YY + 1024],
                                           ar.fa[:, HH:HH + 1024],
                                           ar.wa[:, 4096 + s * 1024:
                                                 4096 + (s + 1) * 1024]),
                                       deps=[tscan] +
                                       ([prev_j] if prev_j else []),
                                       selfwait=True)
                            st["y"] = thc
                        else:
                            thc = p.op(lane,
                                       lambda e, s=s, HH=HH, HC=HC:
                                       e.tensor_mul(
                                           ar.fa[:, HC:HC + 1024],
                                           ar.fa[:, HH:HH + 1024],
                                           ar.wa[:, 4096 + s * 1024:
                                                 4096 + (s + 1) * 1024]),
                                       deps=[tscan], selfwait=True)
                            st["y"] = p.op(lane,
                                           lambda e, YY=YY, HC=HC:
                                           e.tensor_add(
                                               ar.fa[:, YY:YY + 1024],
                                               ar.fa[:, YY:YY + 1024],
                                               ar.fa[:, HC:HC + 1024]),
                                           deps=[thc, st["y"]], selfwait=True)
                        st["h"] = thc
                    else:
                        st["h"] = tscan
            if yout:
                prev_j_v = p.dma("gpsimd",
                                 dr["s_y"][jv * 128:(jv + 1) * 128, tt:tt + 1024],
                                 ar.fa[:, 4096:5120], deps=[st_v["y"]])
                prev_j_g = p.dma("gpsimd",
                                 dr["s_y"][jg * 128:(jg + 1) * 128, tt:tt + 1024],
                                 ar.fa[:, 7168:8192], deps=[st_g["y"]])
            else:
                prev_j_v = st_v["car"]
                prev_j_g = st_g["car"]
        lane_tail = [t for t in (prev_j_v, prev_j_g) if t is not None]
    p.barrier()



def phase_postscan(p, ar, dr, t0, t1):
    wtoks = load_wT_bf16(p, ar.wa, dr["out_projT"], 4096, D)
    tD = p.dma("sync", ar.fs[:, 0:32], dr["Dc"])
    p.barrier()
    grp_tail = None
    prev_mm = [None, None]
    prev_add = None
    prev_st = None
    prev_zj = [None, None]
    for ck in range((t1 - t0) // 512):
        tt = t0 + ck * 512
        tA = p.dma("sync", sb3(ar.fa[:, 0:4096], D),
                   stream_chunk_ap(dr["s_a"], tt), deps=[prev_add, prev_st])
        yf_toks = []
        for grp in range(4):
            g0 = grp * 8
            gdeps = [grp_tail]
            ty = p.dma("sync", sb3(ar.fa[:, 4096:8192], 512),
                       fm_block_ap(dr["s_y"], g0 * 128, 8, tt, 512),
                       deps=gdeps)
            tu = p.dma("scalar", sb3(ar.fb[:, 0:4096], 512),
                       fm_block_ap(dr["s_uc"], g0 * 128, 8, tt, 512),
                       deps=gdeps)
            grp_toks = []
            for jl in range(8):
                j = g0 + jl
                pz = jl % 2
                tzj = p.dma("gpsimd", ar.ms[:, pz * 512:pz * 512 + 512],
                            dr["s_mz"][j * 128:(j + 1) * 128, tt:tt + 512],
                            deps=[prev_zj[pz]])
                t1_ = p.op("vector",
                           lambda e, j=j, jl=jl, pz=pz: e.scalar_tensor_tensor(
                               ar.fs[:, 1024 + pz * 512:1024 + pz * 512 + 512],
                               ar.fb[:, jl * 512:(jl + 1) * 512],
                               ar.fs[:, j:j + 1],
                               ar.fa[:, 4096 + jl * 512:4096 + (jl + 1) * 512],
                               ALU.mult, ALU.add),
                           deps=[ty, tu] +
                           ([grp_toks[-2]] if jl >= 2 else []),
                           selfwait=True)
                t2_ = p.op("scalar",
                           lambda e, pz=pz: e.activation(
                               ar.fs[:, 2048 + pz * 512:2048 + pz * 512 + 512],
                               ar.ms[:, pz * 512:pz * 512 + 512],
                               AF.Silu),
                           deps=[tzj] + ([grp_toks[-2]] if jl >= 2 else []),
                           selfwait=False)
                bdeps = [t1_, t2_]
                if ck > 0 and grp == 0 and jl == 0:
                    bdeps.append(prev_st)
                t3_ = p.op("vector",
                           lambda e, j=j, pz=pz: e.tensor_mul(
                               ar.ba[:, j * 512:(j + 1) * 512],
                               ar.fs[:, 1024 + pz * 512:1024 + pz * 512 + 512],
                               ar.fs[:, 2048 + pz * 512:2048 + pz * 512 + 512]),
                           deps=bdeps, selfwait=True)
                prev_zj[pz] = t2_
                grp_toks.append(t3_)
            grp_tail = grp_toks[-1]
            yf_toks += grp_toks
        adds = []
        for i in range(4):
            for nh in range(2):
                slot = (i * 2 + nh) % 2
                pairs = [(ar.ba[:, m * 512 + i * 128: m * 512 + i * 128 + 128],
                          ar.wa[:, m * 1024 + nh * 512: m * 1024 + nh * 512 + 512])
                         for m in range(32)]
                mdeps = yf_toks + [tA] + wtoks
                if prev_mm[slot] is not None:
                    mdeps.append(prev_mm[slot])
                tmm = mm_accum(p, ar.pa[:, slot * 512:slot * 512 + 512], pairs,
                               deps=mdeps)
                tad = p.op("vector",
                           lambda e, i=i, nh=nh, slot=slot: e.tensor_add(
                               ar.fa[:, i * 1024 + nh * 512:
                                     i * 1024 + nh * 512 + 512],
                               ar.fa[:, i * 1024 + nh * 512:
                                     i * 1024 + nh * 512 + 512],
                               ar.pa[:, slot * 512:slot * 512 + 512]),
                           deps=[tmm], selfwait=True)
                prev_mm[slot] = tad
                adds.append(tad)
        prev_add = adds[-1]
        prev_st = p.dma("scalar", stream_chunk_ap(dr["s_a3"], tt),
                        sb3(ar.fa[:, 0:4096], D), deps=adds)
    p.barrier()



def build_core_program(h=1, limit=99, dump=None):
    import itertools
    _ctr = itertools.count()
    def _go():
        return next(_ctr) < limit
    nc = bass.Bass("TRN2", target_bir_lowering=False)
    p = Prog(nc)
    dr = {}
    ext_in = [
        ("x_in", [T_FULL, D]), ("c_eye", [128, 128]), ("c_ones", [1, 512]),
        ("w_qkvF", [6 * D, 3072]), ("w_outF", [6 * D, D]),
        ("b_outF", [6, D]), ("l0g", [128, 48]), ("l0b", [128, 48]),
        ("gate_wT", [D, 8]), ("W1Tf", [8 * D, 1536]), ("W2Tf", [8 * 768, D]),
        ("Ws1P", [D, 3072]), ("Ws2P", [1536, D]),
        ("g1c", [128, 8]), ("b1c", [128, 8]), ("w1T", [D, 4096]),
        ("bias1", [4096]), ("w2T", [4096, D]), ("bias2", [D]),
        ("ipT", [D, 8192]), ("g2c", [128, 8]), ("b2c", [128, 8]),
        ("conv_wc", [128, 128]), ("conv_bc", [128, 32]),
        ("x_projT", [4096, 96]), ("dt_projT", [64, 4096]),
        ("dt_proj_b", [4096]), ("Alogc", [128, 512]), ("sel16", [32, 4096]),
        ("Dc", [128, 32]), ("out_projT", [4096, D]),
        ("gWT", [D, 15]), ("gb", [15]), ("rW1f", [15 * D, 2048]),
        ("rb1f", [15 * 2048]), ("rW2f", [15 * 2048, D]), ("rb2f", [15 * D]),
        ("sW1m", [D, 4096]), ("sb1", [4096]), ("sW2m", [4096, D]),
        ("sb2", [D]), ("cmask128", [128, 1]), ("cmn128", [128, 1]),
    ]
    for name, shape in ext_in:
        dr[name] = p.dram(name, shape, F32, kind="ExternalInput")
    dr["out"] = p.dram("out", [T_HALF, D], F32, kind="ExternalOutput")
    for name in ("s_x", "s_att", "s_f", "s_z", "s_a", "s_a3"):
        dr[name] = p.dram(name, [T_FULL, D], F32)
    dr["s_fm"] = p.dram("s_fm", [D, T_FULL], BF16)
    dr["s_wd"] = p.dram("s_wd", [T_FULL, 16], F32)
    dr["s_h"] = p.dram("s_h", [4096, T_FULL], BF16)
    dr["s_v"] = p.dram("s_v", [D, T_FULL], BF16)
    for name in ("s_uraw", "s_mz", "s_uc", "s_dtm", "s_y"):
        dr[name] = p.dram(name, [4096, T_FULL], F32)
    dr["s_ndtu"] = p.dram("s_ndtu", [4096, T_FULL], BF16)
    dr["s_ucb"] = p.dram("s_ucb", [4096, T_FULL], BF16)
    dr["s_xdbl"] = p.dram("s_xdbl", [96, T_FULL], F32)

    own0 = T_HALF
    own1 = T_FULL

    ar = Arena(p)
    ar.load_consts(p, dr)
    tcm = p.dma("sync", ar.cf[:, 2:3], dr["cmask128"])
    tcn = p.dma("sync", ar.cf[:, 3:4], dr["cmn128"])
    p.barrier()
    if _go(): phase_copy_stream(p, ar, dr, 0, T_FULL, "x_in", "s_x")
    if _go(): phase_ln_stream(p, ar, dr, 0, T_FULL, src="s_x", dst="s_att")
    for l in range(6):
        if _go(): phase_att_layer(p, ar, dr, l, 0, T_FULL)
    if _go(): phase_add_ln(p, ar, dr, 0, T_FULL, x="s_x", slot="s_att", dst="s_x")
    if _go(): phase_ln_stream(p, ar, dr, 0, T_FULL, src="s_x", dst="s_f")
    if _go(): phase_ffn0(p, ar, dr, 0, T_FULL)
    if _go(): phase_add_ln(p, ar, dr, 0, T_FULL, x="s_x", slot="s_z", dst="s_x")
    if _go(): phase_ln_stream(p, ar, dr, 0, T_FULL, src="s_x", dst="s_f")
    dr2 = dict(dr)
    dr2["s_in"] = dr["s_f"]
    dr2["s_out"] = dr["s_a"]
    if _go(): phase_mlp_a(p, ar, dr2, 0, T_FULL)
    if _go(): phase_mlp_b(p, ar, dr2, 0, T_FULL)
    if _go(): phase_proj_fm(p, ar, dr, 0, T_FULL, "s_a", "s_uraw",
                            dr["ipT"][:, 0:4096],
                            dr["g2c"], dr["b2c"], None, None)
    if _go(): phase_proj_fm(p, ar, dr, 0, T_FULL, "s_a", "s_mz",
                            dr["ipT"][:, 4096:8192],
                            dr["g2c"], dr["b2c"], None, None)
    if _go(): phase_conv(p, ar, dr, 0, T_FULL)
    if _go(): phase_dt(p, ar, dr, 0, T_FULL)
    if _go(): phase_scan_setup(p, ar, dr)
    if _go(): phase_scan(p, ar, dr, 0, T_HALF, yout=False,
                          bscale_ap=ar.cf[:, 3:4])
    if _go(): phase_scan(p, ar, dr, T_HALF, T_FULL, yout=True)
    if _go(): phase_postscan(p, ar, dr, own0, own1)
    if _go(): phase_add_ln(p, ar, dr, own0, own1, x="s_x", slot="s_a3", dst="s_x")
    if _go(): phase_ln_stream(p, ar, dr, own0, own1, src="s_x", dst="s_f")
    if _go(): phase_ffn1(p, ar, dr, own0, own1)
    if _go(): phase_add_ln(p, ar, dr, own0, own1, x="s_x", slot="s_z", dst="out",
                           dst_off=own0)
    if dump is not None:
        shp = [dr[dump].shape[0], dr[dump].shape[1]]
        dr["dbg_out"] = p.dram("dbg_out", shp, dr[dump].dtype,
                               kind="ExternalOutput")
        prev = None
        for r0 in range(0, shp[0], 128):
            t = p.dma("sync", ar.fa[:, 0:shp[1]] if shp[1] <= 8192 else None,
                      dr[dump][r0:r0 + 128, :], deps=[prev] if prev else ())
            prev = p.dma("scalar", dr["dbg_out"][r0:r0 + 128, :],
                         ar.fa[:, 0:shp[1]], deps=[t])
    p.barrier()
    p.replay()
    return nc


_PERM0 = np.concatenate([np.arange(0, 768), np.arange(1536, 2304),
                         np.arange(768, 1536), np.arange(2304, 3072)])


def _percol(v):
    n = v.shape[0] // 128
    return np.ascontiguousarray(v.reshape(n, 128).T)


def _host_prep(inp):
    C = np.ascontiguousarray
    d = {}
    d["c_eye"] = np.eye(128, dtype=np.float32)
    d["c_ones"] = np.ones((1, 512), np.float32)
    d["w_qkvF"] = inp["L0_attn_w_qkv"].reshape(6 * D, 3072)
    d["w_outF"] = inp["L0_attn_w_out"].reshape(6 * D, D)
    d["b_outF"] = inp["L0_attn_b_out"]
    d["l0g"] = np.concatenate([_percol(inp["L0_attn_gamma"][l])
                               for l in range(6)], axis=1)
    d["l0b"] = np.concatenate([_percol(inp["L0_attn_beta"][l])
                               for l in range(6)], axis=1)
    d["gate_wT"] = C(inp["L0_ffn_gate_w"].T)
    d["W1Tf"] = C(inp["L0_ffn_W1"].transpose(0, 2, 1)).reshape(8 * D, 1536)
    d["W2Tf"] = C(inp["L0_ffn_W2"].transpose(0, 2, 1)).reshape(8 * 768, D)
    d["Ws1P"] = C(inp["L0_ffn_Ws1"].T[:, _PERM0])
    d["Ws2P"] = C(inp["L0_ffn_Ws2"].T)
    d["g1c"] = _percol(inp["L1_attn_g1"])
    d["b1c"] = _percol(inp["L1_attn_b1"])
    d["w1T"] = C(inp["L1_attn_w1"].T)
    d["bias1"] = inp["L1_attn_bias1"]
    d["w2T"] = C(inp["L1_attn_w2"].T)
    d["bias2"] = inp["L1_attn_bias2"]
    d["ipT"] = C(inp["L1_attn_in_proj_w"].T)
    d["g2c"] = _percol(inp["L1_attn_g2"])
    d["b2c"] = _percol(inp["L1_attn_b2"])
    d["conv_wc"] = C(inp["L1_attn_conv_w"].reshape(32, 128, 4)
                     .transpose(1, 0, 2).reshape(128, 128))
    d["conv_bc"] = _percol(inp["L1_attn_conv_b"])
    d["x_projT"] = C(inp["L1_attn_x_proj_w"].T)
    d["dt_projT"] = C(inp["L1_attn_dt_proj_w"].T)
    d["dt_proj_b"] = inp["L1_attn_dt_proj_b"]
    d["Alogc"] = C(inp["L1_attn_A_log"].reshape(32, 128, 16)
                   .transpose(1, 0, 2).reshape(128, 512))
    sel = np.zeros((32, 4096), np.float32)
    for g in range(32):
        sel[g, g * 128:(g + 1) * 128] = 1.0
    d["sel16"] = sel
    d["Dc"] = _percol(inp["L1_attn_D"])
    d["out_projT"] = C(inp["L1_attn_out_proj_w"].T)
    d["gWT"] = inp["L1_ffn_gW"]
    d["gb"] = inp["L1_ffn_gb"]
    d["rW1f"] = inp["L1_ffn_rW1"].reshape(15 * D, 2048)
    d["rb1f"] = inp["L1_ffn_rb1"].reshape(-1)
    d["rW2f"] = inp["L1_ffn_rW2"].reshape(15 * 2048, D)
    d["rb2f"] = inp["L1_ffn_rb2"].reshape(-1)
    d["sW1m"] = inp["L1_ffn_sW1"][0]
    d["sb1"] = inp["L1_ffn_sb1"][0]
    d["sW2m"] = inp["L1_ffn_sW2"][0]
    d["sb2"] = inp["L1_ffn_sb2"][0]
    return {k: np.ascontiguousarray(v, dtype=np.float32) for k, v in d.items()}


def kernel(**inputs):
    w = _host_prep(inputs)
    x = np.ascontiguousarray(inputs["L0_attn_x"], dtype=np.float32)
    nc = build_core_program()
    in_maps = []
    for c in range(8):
        b, h = c // 2, c % 2
        m = dict(w)
        if h == 0:
            m["x_in"] = np.ascontiguousarray(
                np.concatenate([x[b, T_HALF:], x[b, :T_HALF]], axis=0))
            m["cmask128"] = np.zeros((128, 1), np.float32)
            m["cmn128"] = np.zeros((128, 1), np.float32)
        else:
            m["x_in"] = x[b]
            m["cmask128"] = np.ones((128, 1), np.float32)
            m["cmn128"] = np.full((128, 1), -1.0, np.float32)
        in_maps.append(m)
    res = run_bass_kernel_spmd(nc, in_maps, core_ids=list(range(8)))
    out = np.empty((4, T_FULL, D), np.float32)
    for c in range(8):
        b, h = c // 2, c % 2
        out[b, h * T_HALF:(h + 1) * T_HALF] = res.results[c]["out"]
    return out
